# version 15
# baseline (speedup 1.0000x reference)
import numpy as np
import concourse.bass as bass
import concourse.mybir as mybir
from concourse.bass_utils import run_bass_kernel_spmd
from contextlib import ExitStack

F32 = mybir.dt.float32
BF16 = mybir.dt.bfloat16
AX = mybir.AxisListType.X
ALU = mybir.AluOpType
ACT = mybir.ActivationFunctionType

D = 2048
T = 2048
HALO = 256
TH = T + HALO
NH = 16
HD = 128
WIN = 256
M = 4096
EXP = 8
R = 64
DC = D // 128
TBS = 512
NTB = T // TBS
NTC = T // 128
NTHC = TH // 128
ISQD = float(1.0 / np.sqrt(D))
ISQH = float(1.0 / np.sqrt(HD))
EPS = 1e-5
P0_TILES = NTHC
HK_JCS = DC


def build_program(debug=False, upto=99):
    nc = bass.Bass("TRN2", target_bir_lowering=False)
    dt_dbg = "ExternalOutput" if debug else "Internal"

    def din(name, shape):
        return nc.dram_tensor(name, shape, F32, kind="ExternalInput").ap()

    x_in = din("x_in", [TH, D])
    wqT_h = din("wqT_h", [NH, D, HD])
    wkT_h = din("wkT_h", [NH, D, HD])
    wvT_h = din("wvT_h", [NH, D, HD])
    wk_nat = din("wk_nat", [D, D])
    woT = din("woT", [D, D])
    mbT_pb = din("mbT_pb", [8, 128, DC, 512])
    mb_pb = din("mb_pb", [DC, 128, 32, 128])
    wvT_pb = din("wvT_pb", [DC, 128, DC, 128])
    W1_nat = din("W1_nat", [D, D])
    b1c = din("b1c", [128, DC])
    W2r = din("W2r", [DC, 128, EXP])
    b2b = din("b2b", [128, EXP])
    gw_rep = din("gw_rep", [128, D])
    gb_rep = din("gb_rep", [128, 1])
    VT_in = din("VT_in", [D, EXP * R])
    UT_in = din("UT_in", [EXP * R, D])
    iden_in = din("iden_in", [128, 128])
    mask3 = din("mask3", [3, 128, 384])

    x_out = nc.dram_tensor("x_out", [T, D], F32, kind="ExternalOutput").ap()

    S_hk = nc.dram_tensor("S_hk", [NTB, DC, 128, TBS], BF16, kind=dt_dbg).ap()
    B_mbT = nc.dram_tensor("B_mbT", [8, 128, DC, 512], BF16, kind="Internal").ap()
    B_mb = nc.dram_tensor("B_mb", [DC, 128, 32, 128], BF16, kind="Internal").ap()
    B_wv = nc.dram_tensor("B_wv", [DC, 128, DC, 128], BF16, kind="Internal").ap()
    B_wo = nc.dram_tensor("B_wo", [DC, 128, D], BF16, kind="Internal").ap()
    S_ctx = nc.dram_tensor("S_ctx", [NTB, NH, 128, 512], BF16, kind=dt_dbg).ap()
    S_ao = nc.dram_tensor("S_ao", [NTB, 4, 128, D], BF16, kind=dt_dbg).ap()
    S_x1 = nc.dram_tensor("S_x1", [NTC, 128, D], BF16, kind=dt_dbg).ap()
    dbg = {}
    if debug:
        dbg["yfm"] = nc.dram_tensor("D_yfm", [DC, 128, TH], BF16, kind="ExternalOutput").ap()
        dbg["sim0"] = nc.dram_tensor("D_sim0", [4, 128, M], F32, kind="ExternalOutput").ap()
        dbg["p1_0"] = nc.dram_tensor("D_p1_0", [DC, 128, TBS], BF16, kind="ExternalOutput").ap()
        dbg["p2_0"] = nc.dram_tensor("D_p2_0", [DC, 128, TBS], BF16, kind="ExternalOutput").ap()
        dbg["rs"] = nc.dram_tensor("D_rs", [128, NTC], F32, kind="ExternalOutput").ap()
        dbg["alpha"] = nc.dram_tensor("D_alpha", [128, NTC * EXP], F32, kind="ExternalOutput").ap()
        dbg["y2fm"] = nc.dram_tensor("D_y2fm", [DC, 128, T], BF16, kind="ExternalOutput").ap()

    es = ExitStack()
    open_stacks = []

    def new_stack():
        st = ExitStack()
        open_stacks.append(st)
        return st

    def close_stack(st):
        assert open_stacks and open_stacks[-1] is st, "non-LIFO stack close"
        open_stacks.pop()
        st.close()

    def close_all():
        while open_stacks:
            open_stacks.pop().close()

    with es:
        def sbuf(name, shape, dt, stack=None):
            return (stack or es).enter_context(nc.sbuf_tensor(name, shape, dt))

        iden_bf = sbuf("iden_bf", [128, 128], BF16)
        masks_bf = sbuf("masks_bf", [128, 3 * 384], BF16)
        b1_sb = sbuf("b1_sb", [128, DC], F32)
        W2_bf = sbuf("W2_bf", [128, DC * EXP], BF16)
        b2_sb = sbuf("b2_sb", [128, EXP], F32)
        gw_bf = sbuf("gw_bf", [128, D], BF16)
        gb_sb = sbuf("gb_sb", [128, 1], F32)
        rowsums = sbuf("rowsums", [128, NTC], F32)
        alpha_sb = sbuf("alpha_sb", [128, NTC * EXP], F32)
        stat = sbuf("stat", [128, 16], F32)
        top8 = sbuf("top8", [128, 8], F32)
        ones_bf = sbuf("ones_bf", [128, 1], BF16)
        onesr_bf = sbuf("onesr_bf", [1, 128], BF16)
        gv_sb = sbuf("gv_sb", [128, DC], F32)
        gv_bf = sbuf("gv_bf", [128, DC], BF16)
        bp_row = sbuf("bp_row", [1, 512], F32)
        rs_row = sbuf("rs_row", [1, 512], F32)
        scl_row = sbuf("scl_row", [1, 512], F32)
        sclb_row = sbuf("sclb_row", [1, 512], BF16)

        psA = es.enter_context(nc.psum_tensor("psA", [128, 512], F32))
        psB = es.enter_context(nc.psum_tensor("psB", [128, 512], F32))
        psC = es.enter_context(nc.psum_tensor("psC", [128, 512], F32))
        psD = es.enter_context(nc.psum_tensor("psD", [128, 512], F32))
        psE = es.enter_context(nc.psum_tensor("psE", [128, 512], F32))
        psF = es.enter_context(nc.psum_tensor("psF", [128, 512], F32))
        psT0 = es.enter_context(nc.psum_tensor("psT0", [128, 512], F32))
        psT1 = es.enter_context(nc.psum_tensor("psT1", [128, 512], F32))
        psAB = [psA, psB]
        psCD = [psC, psD]
        psEF = [psE, psF]
        psT = [psT0, psT1]
        PSNAME = {id(psA): "psA", id(psB): "psB", id(psC): "psC", id(psD): "psD",
                  id(psE): "psE", id(psF): "psF", id(psT0): "psT0", id(psT1): "psT1"}

        gv_state = {"done": 0}
        H = {s: es.enter_context(nc.semaphore(s)) for s in ("T", "V", "C")}
        C = {s: 0 for s in H}
        ENG_SEM = {"tensor": "T", "vector": "V", "scalar": "C", "gpsimd": None, "sync": None}
        block = es.enter_context(nc.Block())

        def sem_of(s):
            if s not in H:
                H[s] = es.enter_context(nc.semaphore(f"d_{s}"))
                C[s] = 0
            return H[s]

        def inc(instr, s, n):
            instr.then_inc(sem_of(s), n)
            C[s] += n
            return C[s]

        def w(eng, s, v):
            if v > 0:
                eng.wait_ge(sem_of(s), v)

        def vchain(v, ins):
            c = inc(ins, "V", 1)
            v.wait_ge(H["V"], c)

        def tr_batch(items, banks, evac="vector", pre=None, dst_guards=None):
            gd = {}
            for k in (dst_guards or []):
                for s_, v_ in (readers.pop(k, None) or {}).items():
                    gd[s_] = max(gd.get(s_, 0), v_)
            eng_first = {"vector": True, "scalar": True}
            done = {}
            lastT = 0
            ncyc = (len(items) + 3) // 4
            for ci in range(ncyc):
                grp = items[ci * 4:(ci + 1) * 4]
                b = banks[ci % len(banks)]

                def tbody(t, grp=grp, b=b, first=(ci == 0)):
                    if first and pre:
                        for s_, v_ in pre:
                            w(t, s_, v_)
                    guard(t, bname(b))
                    for c, (srcap, dstap) in enumerate(grp):
                        ins = t.matmul(b[:, c * 128:(c + 1) * 128], srcap, iden_bf[:],
                                       start=True, stop=True)
                    inc(ins, "T", 1)
                block.tensor(tbody)
                lastT = C["T"]

                engname = ("vector" if (ci % 2 == 0) else "scalar") if evac == "alt" else evac
                sem = "V" if engname == "vector" else "C"

                def ebody(e, grp=grp, b=b, mm=lastT, engname=engname, sem=sem):
                    w(e, "T", mm)
                    if eng_first[engname]:
                        eng_first[engname] = False
                        for s_, v_ in gd.items():
                            w(e, s_, v_)
                    ins = None
                    for c, (srcap, dstap) in enumerate(grp):
                        if engname == "vector":
                            ins = e.tensor_copy(dstap, b[:, c * 128:(c + 1) * 128])
                        else:
                            ins = e.copy(dstap, b[:, c * 128:(c + 1) * 128])
                    inc(ins, sem, 1)
                    reads(bname(b), sem, C[sem])
                getattr(block, engname)(ebody)
                done[sem] = C[sem]
            return {"lastT": lastT, "done": done}

        readers = {}

        def reads(key, s, v):
            d = readers.setdefault(key, {})
            d[s] = max(d.get(s, 0), v)

        def guard(eng, key):
            d = readers.pop(key, None)
            if d:
                for s, v in d.items():
                    w(eng, s, v)

        def bname(ps):
            return PSNAME[id(ps)]

        def barrier():
            snap = dict(C)
            for engname in ("tensor", "vector", "scalar", "gpsimd", "sync"):
                own = ENG_SEM[engname]

                def body(e, snap=snap, own=own):
                    for s, v in snap.items():
                        if s != own and v > 0:
                            e.wait_ge(H[s], v)
                getattr(block, engname)(body)

        es_pc = new_stack()
        pcb = [sbuf(f"pcb{i}", [128, 4096], BF16, es_pc) for i in range(2)]
        pc_units = ([("mbT", (j, hh)) for j in range(8) for hh in range(2)] +
                    [("mb", j) for j in range(DC)] +
                    [("wv", j) for j in range(DC)] +
                    [("wo", j) for j in range(DC)])
        pc_state = {"i": 0, "s": 0}
        pc_ld = {}

        def pc_pump():
            i = pc_state["s"]
            if i < pc_state["i"]:
                kind, j = pc_units[i]
                par = i % 2
                nsz = {"mbT": 4096, "mb": 4096, "wv": 2048, "wo": 2048}[kind]
                if kind == "mbT":
                    jj, hh = j
                    dst = B_mbT[jj].rearrange("p c n -> p (c n)")[:, hh * 4096:(hh + 1) * 4096]
                else:
                    dst = {"mb": lambda: B_mb[j].rearrange("p c n -> p (c n)"),
                           "wv": lambda: B_wv[j].rearrange("p c n -> p (c n)"),
                           "wo": lambda: B_wo[j]}[kind]()

                def sbody(y, i=i, par=par, nsz=nsz, dst=dst):
                    w(y, f"ld_pcb{par}", pc_ld[i])
                    inc(y.dma_start(dst, pcb[par][:, :nsz]), f"st_pcb{par}", 16)
                    reads(f"pcb{par}", f"st_pcb{par}", C[f"st_pcb{par}"])
                block.sync(sbody)
                pc_state["s"] = i + 1
            i = pc_state["i"]
            if i < len(pc_units):
                kind, j = pc_units[i]
                par = i % 2
                if kind == "mbT":
                    jj, hh = j
                    src_ap = mbT_pb[jj].rearrange("p c n -> p (c n)")[:, hh * 4096:(hh + 1) * 4096]
                else:
                    src_ap = {"mb": lambda: mb_pb[j].rearrange("p c n -> p (c n)"),
                              "wv": lambda: wvT_pb[j].rearrange("p c n -> p (c n)"),
                              "wo": lambda: woT[j * 128:(j + 1) * 128, :]}[kind]()
                nsz = {"mbT": 4096, "mb": 4096, "wv": 2048, "wo": 2048}[kind]

                def lbody(g, i=i, par=par, src_ap=src_ap, nsz=nsz):
                    guard(g, f"pcb{par}")
                    pc_ld[i] = inc(g.dma_start(pcb[par][:, :nsz], src_ap), f"ld_pcb{par}", 16)
                block.gpsimd(lbody)
                pc_state["i"] = i + 1

        def pc_drain():
            while pc_state["s"] < len(pc_units):
                pc_pump()

        @block.gpsimd
        def _(g):
            inc(g.dma_start(iden_bf[:], iden_in), "consts", 16)
            inc(g.dma_start(masks_bf[:].rearrange("p (i n) -> p i n", n=384), mask3.rearrange("i p n -> p i n")), "consts", 16)
            inc(g.dma_start(b1_sb[:], b1c), "consts", 16)
            inc(g.dma_start(W2_bf[:].rearrange("p (c k) -> p c k", k=EXP), W2r.rearrange("c p k -> p c k")), "consts", 16)
            inc(g.dma_start(b2_sb[:], b2b), "consts", 16)
            inc(g.dma_start(gw_bf[:], gw_rep), "consts", 16)
            inc(g.dma_start(gb_sb[:], gb_rep), "consts", 16)
        consts_ld = C["consts"]

        @block.vector
        def _(v):
            v.memset(ones_bf[:], 1.0)
            inc(v.memset(onesr_bf[:], 1.0), "V", 1)
        ones_done = C["V"]

        es_y = new_stack()
        es0 = new_stack()
        y_fm = sbuf("y_fm", [128, DC * TH], BF16, es_y)
        xt = [sbuf(f"xt{i}", [128, D], F32, es0) for i in range(2)]
        junk = sbuf("junk", [128, D], F32, es0)
        ybf = [sbuf(f"ybf{i}", [128, D], BF16, es0) for i in range(2)]

        for thc in range(P0_TILES):
            par = thc % 2

            @block.gpsimd
            def _(g):
                guard(g, f"xt{par}")
                inc(g.dma_start(xt[par][:], x_in[thc * 128:(thc + 1) * 128, :]), f"ld_xt{par}", 16)
            x_ld = C[f"ld_xt{par}"]
            pc_pump()

            @block.vector
            def _(v):
                w(v, f"ld_xt{par}", x_ld)
                vchain(v, v.reduce_sum(stat[:, 0:1], xt[par][:], axis=AX))
                inc(v.tensor_scalar_mul(stat[:, 1:2], stat[:, 0:1], 1.0 / D), "V", 1)
            mean_done = C["V"]

            @block.scalar
            def _(s):
                w(s, f"ld_xt{par}", x_ld)
                guard(s, "st23")
                inc(s.activation(junk[:], xt[par][:], ACT.Square, accum_out=stat[:, 2:3]), "C", 1)
            sq_done = C["C"]

            @block.vector
            def _(v):
                w(v, "C", sq_done)
                w(v, "V", mean_done)
                vchain(v, v.tensor_scalar(stat[:, 4:5], stat[:, 1:2], stat[:, 1:2], EPS,
                                          op0=ALU.mult, op1=ALU.subtract))
                inc(v.tensor_scalar(stat[:, 5:6], stat[:, 2:3], 1.0 / D, stat[:, 4:5],
                                    op0=ALU.mult, op1=ALU.subtract), "V", 1)
                reads("st23", "V", C["V"])
            var_done = C["V"]

            @block.scalar
            def _(s):
                w(s, "V", var_done)
                guard(s, "st67")
                inc(s.activation(stat[:, 6:7], stat[:, 5:6], ACT.Sqrt), "C", 1)
            sd_done = C["C"]

            @block.vector
            def _(v):
                w(v, "C", sd_done)
                guard(v, f"ybf{par}")
                vchain(v, v.reciprocal(stat[:, 7:8], stat[:, 6:7]))
                ins = v.tensor_scalar(ybf[par][:], xt[par][:], stat[:, 1:2], stat[:, 7:8],
                                      op0=ALU.subtract, op1=ALU.mult)
                inc(ins, "V", 1)
                reads(f"xt{par}", "V", C["V"])
                reads("st67", "V", C["V"])
            norm_done = C["V"]

            pre0 = [("V", norm_done)]
            if thc == 0:
                pre0.append(("consts", consts_ld))
            items = [(ybf[par][:, dc * 128:(dc + 1) * 128],
                      y_fm[:, dc * TH + thc * 128: dc * TH + thc * 128 + 128])
                     for dc in range(DC)]
            r0 = tr_batch(items, [psT0, psT1, psE, psF], pre=pre0)
            reads(f"ybf{par}", "T", r0["lastT"])

        close_stack(es0)
        if upto <= 0:

            _dump_yfm(block, H, C, inc, y_fm, dbg)
            _finish(block, H, C)
            close_all()
            return nc
        barrier()

        es_hk = new_stack()
        wk_bf = sbuf("wk_bf", [128, DC * 1024], BF16, es_hk)
        hko = [sbuf(f"hko{i}", [128, 512], BF16, es_hk) for i in range(2)]

        def emit_wk_half(half):
            @block.gpsimd
            def _(g):
                guard(g, "wk_bf")
                for dc in range(DC):
                    inc(g.dma_start(wk_bf[:, dc * 1024:(dc + 1) * 1024],
                                    wk_nat[dc * 128:(dc + 1) * 128,
                                           half * 1024:(half + 1) * 1024]), "ld_wk", 16)
            return C["ld_wk"]

        wk_ld = [emit_wk_half(0), None]

        def emit_hk_group(it):
            jc = it // NTB
            tb = it % NTB
            jloc = jc % 8
            par = it % 2
            pp = psAB[par]

            @block.tensor
            def _(t):
                w(t, "ld_wk", wk_ld[jc // 8])
                guard(t, bname(pp))
                for dc in range(DC):
                    ins = t.matmul(pp[:],
                                   wk_bf[:, dc * 1024 + jloc * 128: dc * 1024 + jloc * 128 + 128],
                                   y_fm[:, dc * TH + HALO + tb * TBS: dc * TH + HALO + (tb + 1) * TBS],
                                   start=(dc == 0), stop=(dc == DC - 1))
                inc(ins, "T", 1)
                if it == 8 * NTB - 1:
                    reads("wk_bf", "T", C["T"])
            mm = C["T"]

            @block.vector
            def _(v):
                w(v, "T", mm)
                guard(v, f"hko{par}")
                inc(v.tensor_copy(hko[par][:], pp[:]), "V", 1)
                reads(bname(pp), "V", C["V"])
            ev = C["V"]

            @block.sync
            def _(y):
                y.wait_ge(H["V"], ev)
                inc(y.dma_start(S_hk[tb, jc], hko[par][:]), f"st_hko{par}", 16)
                reads(f"hko{par}", f"st_hko{par}", C[f"st_hko{par}"])

        hkq = []
        for it in range(8 * NTB):
            hkq.append(lambda it=it: emit_hk_group(it))
        hkq.append(lambda: wk_ld.__setitem__(1, emit_wk_half(1)))
        for it in range(8 * NTB, HK_JCS * NTB):
            hkq.append(lambda it=it: emit_hk_group(it))

        if upto <= 1:
            while hkq:
                hkq.pop(0)()
            close_stack(es_hk)
            _dump_yfm(block, H, C, inc, y_fm, dbg)
            _finish(block, H, C)
            close_all()
            return nc

        es2 = new_stack()
        wsl = {nm: [sbuf(f"wsl_{nm}{i}", [128, D], BF16, es2) for i in range(2)]
               for nm in ("q", "k", "v")}
        qf = [sbuf(f"qf{i}", [128, T], BF16, es2) for i in range(2)]
        kf = [sbuf(f"kf{i}", [128, TH], BF16, es2) for i in range(2)]
        vf = sbuf("vf", [128, TH], BF16, es2)
        vtm = [sbuf(f"vtm{i}", [128, TH], BF16, es2) for i in range(2)]
        p_e = [sbuf(f"p_e{i}", [128, 384], BF16, es2) for i in range(2)]
        p_m = [sbuf(f"p_m{i}", [128, 384], BF16, es2) for i in range(2)]
        at_bf = [sbuf(f"at_bf{i}", [128, 384], BF16, es2) for i in range(2)]
        ctxo = [sbuf(f"ctxo{i}", [128, 512], BF16, es2) for i in range(2)]
        rs1 = sbuf("rs1", [128, 4], F32, es2)

        wsl_ld = {}

        def head_srcs(h):
            return [(nm, src) for nm, src in zip(
                ("q", "k", "v"),
                [wqT_h[h].rearrange("(c p) o -> p c o", p=128),
                 wkT_h[h].rearrange("(c p) o -> p c o", p=128),
                 wvT_h[h].rearrange("(c p) o -> p c o", p=128)])]

        @block.gpsimd
        def _(g):
            for h in (0, 1):
                for nm, src in head_srcs(h):
                    guard(g, f"wsl_{nm}{h % 2}")
                    inc(g.dma_start(wsl[nm][h % 2][:].rearrange("p (c o) -> p c o", o=128), src),
                        f"ld_wsl{h % 2}", 16)
                wsl_ld[h] = C[f"ld_wsl{h % 2}"]

        qkv_ev = {}
        QKV_GROUPS = ([("q", tb) for tb in range(4)] +
                      [("k", tb) for tb in range(5)] +
                      [("v", tb) for tb in range(5)])
        WIDTHS = {"q": T, "k": TH, "v": TH}
        SRCOFF = {"q": HALO, "k": 0, "v": 0}

        def emit_qkv_group(h, gi):
            par = h % 2
            nm, tb = QKV_GROUPS[gi]
            dst = {"q": qf[par], "k": kf[par], "v": vf}[nm]
            pp = psEF[gi % 2]
            n = min(512, WIDTHS[nm] - tb * 512)

            @block.tensor
            def _(t):
                if gi == 0:
                    w(t, f"ld_wsl{par}", wsl_ld[h])
                    guard(t, f"qf{par}")
                    guard(t, f"kf{par}")
                if nm == "v" and tb == 0:
                    guard(t, "vf")
                guard(t, bname(pp))
                for dc in range(DC):
                    ins = t.matmul(pp[:, :n], wsl[nm][par][:, dc * 128:(dc + 1) * 128],
                                   y_fm[:, dc * TH + SRCOFF[nm] + tb * 512:
                                        dc * TH + SRCOFF[nm] + tb * 512 + n],
                                   start=(dc == 0), stop=(dc == DC - 1))
                inc(ins, "T", 1)
                if gi == len(QKV_GROUPS) - 1:
                    reads(f"wsl_q{par}", "T", C["T"])
                    reads(f"wsl_k{par}", "T", C["T"])
                    reads(f"wsl_v{par}", "T", C["T"])
            mm = C["T"]

            @block.vector
            def _(v):
                w(v, "T", mm)
                inc(v.tensor_copy(dst[:, tb * 512: tb * 512 + n], pp[:, :n]), "V", 1)
                reads(bname(pp), "V", C["V"])
            qkv_ev[(h, nm, tb)] = ("V", C["V"])

        def emit_attn_pair(h, qt0, fillers=()):
            par = h % 2
            qt1 = qt0 + 1
            fillers = list(fillers)

            @block.tensor
            def _(t):
                w(t, *qkv_ev[(h, "q", (qt1 * 128) // 512)])
                w(t, *qkv_ev[(h, "k", min((qt1 * 128 + 383) // 512, 4))])
                guard(t, bname(psA))
                t.matmul(psA[:, :384], qf[par][:, qt0 * 128:(qt0 + 1) * 128],
                         kf[par][:, qt0 * 128: qt0 * 128 + 384], start=True, stop=True)
                guard(t, bname(psB))
                ins = t.matmul(psB[:, :384], qf[par][:, qt1 * 128:(qt1 + 1) * 128],
                               kf[par][:, qt1 * 128: qt1 * 128 + 384], start=True, stop=True)
                inc(ins, "T", 1)
                if qt1 == NTC - 1:
                    reads(f"qf{par}", "T", C["T"])
                    reads(f"kf{par}", "T", C["T"])
            sc = C["T"]
            if fillers and fillers[0][0] == "qkv":
                fillers.pop(0)[1]()

            @block.scalar
            def _(s):
                w(s, "T", sc)
                guard(s, "p_e0")
                guard(s, "p_e1")
                s.activation(p_e[0][:], psA[:, :384], ACT.Exp, scale=ISQH)
                ins = s.activation(p_e[1][:], psB[:, :384], ACT.Exp, scale=ISQH)
                inc(ins, "C", 1)
                reads(bname(psA), "C", C["C"])
                reads(bname(psB), "C", C["C"])
            ex = C["C"]

            mask0 = masks_bf[:, (qt0 if qt0 < 2 else 2) * 384:(qt0 if qt0 < 2 else 2) * 384 + 384]
            mask1 = masks_bf[:, (qt1 if qt1 < 2 else 2) * 384:(qt1 if qt1 < 2 else 2) * 384 + 384]

            @block.vector
            def _(v):
                w(v, "C", ex)
                guard(v, "p_m0")
                guard(v, "p_m1")
                v.scalar_tensor_tensor(p_m[0][:], p_e[0][:], 0.0, mask0,
                                       op0=ALU.add, op1=ALU.mult, accum_out=rs1[:, 0:1])
                vchain(v, v.scalar_tensor_tensor(p_m[1][:], p_e[1][:], 0.0, mask1,
                                                 op0=ALU.add, op1=ALU.mult,
                                                 accum_out=rs1[:, 2:3]))
                v.reciprocal(rs1[:, 1:2], rs1[:, 0:1])
                vchain(v, v.reciprocal(rs1[:, 3:4], rs1[:, 2:3]))
                v.tensor_scalar_mul(p_m[0][:], p_m[0][:], rs1[:, 1:2])
                ins = v.tensor_scalar_mul(p_m[1][:], p_m[1][:], rs1[:, 3:4])
                inc(ins, "V", 1)
                reads("p_e0", "V", C["V"])
                reads("p_e1", "V", C["V"])
            pn = C["V"]

            items = [(p_m[0][:, c * 128:(c + 1) * 128], at_bf[0][:, c * 128:(c + 1) * 128])
                     for c in range(3)] +                     [(p_m[1][:, c * 128:(c + 1) * 128], at_bf[1][:, c * 128:(c + 1) * 128])
                     for c in range(3)]
            ra = tr_batch(items, [psT0, psT1], pre=[("V", pn)],
                          dst_guards=["at_bf0", "at_bf1"])
            reads("p_m0", "T", ra["lastT"])
            reads("p_m1", "T", ra["lastT"])
            while fillers:
                fillers.pop(0)[1]()

            @block.tensor
            def _(t):
                for s_, v_ in ra["done"].items():
                    w(t, s_, v_)
                guard(t, bname(psC))
                for c in range(3):
                    t.matmul(psC[:, :128], vtm[par][:, (qt0 + c) * 128:(qt0 + c + 1) * 128],
                             at_bf[0][:, c * 128:(c + 1) * 128],
                             start=(c == 0), stop=(c == 2))
                guard(t, bname(psD))
                for c in range(3):
                    ins = t.matmul(psD[:, :128], vtm[par][:, (qt1 + c) * 128:(qt1 + c + 1) * 128],
                                   at_bf[1][:, c * 128:(c + 1) * 128],
                                   start=(c == 0), stop=(c == 2))
                inc(ins, "T", 1)
                reads("at_bf0", "T", C["T"])
                reads("at_bf1", "T", C["T"])
                if qt1 >= NTC - 2:
                    reads(f"vtm{par}", "T", C["T"])
            cm = C["T"]

            tb = qt0 // 4
            tpar = tb % 2

            @block.vector
            def _(v):
                w(v, "T", cm)
                if qt0 % 4 == 0:
                    guard(v, f"ctxo{tpar}")
                v.tensor_copy(ctxo[tpar][:, (qt0 % 4) * 128:(qt0 % 4) * 128 + 128], psC[:, :128])
                ins = v.tensor_copy(ctxo[tpar][:, (qt1 % 4) * 128:(qt1 % 4) * 128 + 128],
                                    psD[:, :128])
                inc(ins, "V", 1)
                reads(bname(psC), "V", C["V"])
                reads(bname(psD), "V", C["V"])
            ce = C["V"]

            if qt1 % 4 == 3:
                @block.sync
                def _(y):
                    y.wait_ge(H["V"], ce)
                    inc(y.dma_start(S_ctx[tb, h], ctxo[tpar][:]), f"st_ctxo{tpar}", 16)
                    reads(f"ctxo{tpar}", f"st_ctxo{tpar}", C[f"st_ctxo{tpar}"])
            pc_pump()

        for gi in range(len(QKV_GROUPS)):
            emit_qkv_group(0, gi)

        for h in range(NH):
            par = h % 2

            if h + 2 < NH:
                @block.gpsimd
                def _(g):
                    for nm, src in head_srcs(h + 2):
                        guard(g, f"wsl_{nm}{par}")
                        inc(g.dma_start(wsl[nm][par][:].rearrange("p (c o) -> p c o", o=128), src),
                            f"ld_wsl{par}", 16)
                    wsl_ld[h + 2] = C[f"ld_wsl{par}"]

            items = [(vf[:, kc * 128:(kc + 1) * 128], vtm[par][:, kc * 128:(kc + 1) * 128])
                     for kc in range(NTHC)]
            rv = tr_batch(items, [psT0, psT1],
                          pre=[qkv_ev[(h, "v", 4)]], dst_guards=[f"vtm{par}"])
            reads("vf", "T", rv["lastT"])

            nxt = list(range(len(QKV_GROUPS))) if h + 1 < NH else []
            for qp in range(NTC // 2):
                fill = []
                for _ in range(2):
                    if nxt:
                        gi = nxt.pop(0)
                        fill.append(("qkv", lambda h1=h + 1, gi=gi: emit_qkv_group(h1, gi)))
                for _ in range(2):
                    if hkq:
                        fill.append(("hk", hkq.pop(0)))
                emit_attn_pair(h, 2 * qp, fill)

        while hkq:
            hkq.pop(0)()
        pc_drain()
        _dump_yfm(block, H, C, inc, y_fm, dbg)
        close_stack(es2)
        close_stack(es_hk)
        close_stack(es_y)
        close_stack(es_pc)
        if upto <= 2:
            _finish(block, H, C)
            close_all()
            return nc
        barrier()

        tb_snap = {"w": None}
        for tb in range(NTB):
            es5o = new_stack()
            es5a = new_stack()
            w_T = sbuf(f"w_T_{tb}", [128, 32 * 512], BF16, es5o)
            p1_fm = sbuf(f"p1_fm_{tb}", [128, DC * 512], BF16, es5o)
            p2_fm = sbuf(f"p2_fm_{tb}", [128, DC * 512], BF16, es5o)
            hk_sl = sbuf(f"hk_sl_{tb}", [128, DC * TBS], BF16, es5a)
            sim_sb = [sbuf(f"sim{i}_{tb}", [128, M], F32, es5a) for i in range(4)]
            es5m = new_stack()
            mbT_sl = [sbuf(f"mbT_sl{i}_{tb}", [128, DC * 512], BF16, es5m) for i in range(2)]

            @block.gpsimd
            def _(g):
                if tb_snap["w"]:
                    for s_, v_ in tb_snap["w"]:
                        w(g, s_, v_)
                guard(g, "hk_sl")
                inc(g.dma_start(hk_sl[:].rearrange("p (c n) -> p c n", n=TBS),
                                S_hk[tb].rearrange("c p n -> p c n")), "ld_hksl", 16)
            hk_ld = C["ld_hksl"]

            sim_ev = {}
            for mb in range(8):
                par = mb % 2

                @block.gpsimd
                def _(g):
                    guard(g, f"mbT_sl{par}")
                    inc(g.dma_start(mbT_sl[par][:].rearrange("p (c n) -> p c n", n=512),
                                    B_mbT[mb]), f"ld_mbT{par}", 16)
                mbT_ld = (f"ld_mbT{par}", C[f"ld_mbT{par}"])

                for tsub in range(4):
                    pp = psAB[tsub % 2]

                    @block.tensor
                    def _(t):
                        if mb == 0 and tsub == 0:
                            w(t, "ld_hksl", hk_ld)
                        w(t, mbT_ld[0], mbT_ld[1])
                        guard(t, bname(pp))
                        for jc in range(DC):
                            ins = t.matmul(pp[:],
                                     hk_sl[:, jc * TBS + tsub * 128: jc * TBS + tsub * 128 + 128],
                                     mbT_sl[par][:, jc * 512:(jc + 1) * 512],
                                     start=(jc == 0), stop=(jc == DC - 1))
                        inc(ins, "T", 1)
                        if tsub == 3:
                            reads(f"mbT_sl{par}", "T", C["T"])
                        if mb == 7 and tsub == 3:
                            reads("hk_sl", "T", C["T"])
                    mm = C["T"]

                    @block.vector
                    def _(v):
                        w(v, "T", mm)
                        guard(v, f"sim{tsub}_{mb}")
                        inc(v.tensor_scalar_mul(sim_sb[tsub][:, mb * 512:(mb + 1) * 512],
                                                pp[:], ISQD), "V", 1)
                        reads(bname(pp), "V", C["V"])
                    sim_ev[(mb, tsub)] = C["V"]

            if debug and tb == 0:
                @block.sync
                def _(y):
                    for tsub in range(4):
                        y.wait_ge(H["V"], sim_ev[(7, tsub)])
                        inc(y.dma_start(dbg["sim0"][tsub], sim_sb[tsub][:]), "st_dbg", 16)
                        for _mb in range(8):
                            reads(f"sim{tsub}_{_mb}", "st_dbg", C["st_dbg"])

            close_stack(es5m)
            mfree = {"T": C["T"], "V": C["V"]}
            es5e = new_stack()
            e_f = sbuf(f"e_f_{tb}", [128, M], F32, es5e)
            w_bf = sbuf(f"w_bf_{tb}", [128, M], BF16, es5e)

            wtr_cp = {}
            for tsub in range(4):
                tc = tb * 4 + tsub

                @block.vector
                def _(v):
                    w(v, "V", sim_ev[(7, tsub)])
                    guard(v, "top8")
                    inc(v.max(out=top8[:], in_=sim_sb[tsub][:]), "V", 1)
                t8_done = C["V"]

                @block.scalar
                def _(s):
                    w(s, "V", sim_ev[(7, tsub)])
                    if tsub == 0:
                        for s_, v_ in mfree.items():
                            w(s, s_, v_)
                    guard(s, "e_f")
                    inc(s.activation(e_f[:], sim_sb[tsub][:], ACT.Exp), "C", 1)
                ee_done = C["C"]

                @block.vector
                def _(v):
                    w(v, "C", ee_done)
                    w(v, "V", t8_done)
                    guard(v, "w_bf")
                    ins = v.scalar_tensor_tensor(w_bf[:], sim_sb[tsub][:], top8[:, 7:8], e_f[:],
                                                 op0=ALU.is_ge, op1=ALU.mult,
                                                 accum_out=rowsums[:, tc:tc + 1])
                    inc(ins, "V", 1)
                    reads("e_f", "V", C["V"])
                    reads("top8", "V", C["V"])
                    for _mb in range(8):
                        reads(f"sim{tsub}_{_mb}", "V", C["V"])
                stt_done = C["V"]

                items = [(w_bf[:, mc * 128:(mc + 1) * 128],
                          w_T[:, mc * 512 + tsub * 128: mc * 512 + tsub * 128 + 128])
                         for mc in range(32)]
                rw = tr_batch(items, [psT0, psT1, psC, psD],
                              pre=[("V", stt_done)],
                              dst_guards=(["w_T"] if tsub == 0 else None))
                reads("w_bf", "T", rw["lastT"])
                wtr_cp[tsub] = dict(rw["done"])

            close_stack(es5e)
            close_stack(es5a)
            free_snap = {"T": C["T"], "V": C["V"], "C": C["C"]}
            es5b = new_stack()
            mb_sl = [sbuf(f"mb_sl{i}_{tb}", [128, M], BF16, es5b) for i in range(2)]
            wv_sl = [sbuf(f"wv_sl{i}_{tb}", [128, D], BF16, es5b) for i in range(2)]

            p1_ev = {}
            for dc in range(DC):
                par = dc % 2

                @block.gpsimd
                def _(g):
                    if dc in (0, 1):
                        for s_, v_ in free_snap.items():
                            w(g, s_, v_)
                    guard(g, f"mb_sl{par}")
                    inc(g.dma_start(mb_sl[par][:].rearrange("p (a c) -> p a c", c=128),
                                    B_mb[dc]), f"ld_mb{par}", 16)
                mb_ld = (f"ld_mb{par}", C[f"ld_mb{par}"])

                pp = psAB[par]

                @block.tensor
                def _(t):
                    w(t, mb_ld[0], mb_ld[1])
                    if dc == 0:
                        for s_, v_ in wtr_cp[3].items():
                            w(t, s_, v_)
                        guard(t, "p1_full")
                    guard(t, bname(pp))
                    for mc in range(32):
                        ins = t.matmul(pp[:], mb_sl[par][:, mc * 128:(mc + 1) * 128],
                                 w_T[:, mc * 512:(mc + 1) * 512],
                                 start=(mc == 0), stop=(mc == 31))
                    inc(ins, "T", 1)
                    reads(f"mb_sl{par}", "T", C["T"])
                    if dc == DC - 1:
                        reads("w_T", "T", C["T"])
                mm = C["T"]

                @block.vector
                def _(v):
                    w(v, "T", mm)
                    inc(v.tensor_copy(p1_fm[:, dc * 512:(dc + 1) * 512], pp[:]), "V", 1)
                    reads(bname(pp), "V", C["V"])
                p1_ev[dc] = C["V"]

            @block.tensor
            def _(t):
                w(t, "V", ones_done)
                for gg, vv in wtr_cp[3].items():
                    w(t, gg, vv)
                guard(t, bname(psF))
                for mc in range(32):
                    ins = t.matmul(psF[0:1, 0:512], ones_bf[:], w_T[:, mc * 512:(mc + 1) * 512],
                                   start=(mc == 0), stop=(mc == 31))
                inc(ins, "T", 1)
            rsr_mm = C["T"]

            @block.vector
            def _(v):
                w(v, "T", rsr_mm)
                guard(v, "rs_row")
                inc(v.tensor_copy(rs_row[0:1, :], psF[0:1, 0:512]), "V", 1)
                reads(bname(psF), "V", C["V"])
            rsr_ev = C["V"]

            p2_ev = {}
            for d2c in range(DC):
                par = d2c % 2

                @block.gpsimd
                def _(g):
                    guard(g, f"wv_sl{par}")
                    inc(g.dma_start(wv_sl[par][:].rearrange("p (c o) -> p c o", o=128),
                                    B_wv[d2c]), f"ld_wv{par}", 16)
                wv_ld = (f"ld_wv{par}", C[f"ld_wv{par}"])

                pp = psCD[par]

                @block.tensor
                def _(t):
                    w(t, wv_ld[0], wv_ld[1])
                    w(t, "V", p1_ev[DC - 1])
                    if d2c == 0:
                        guard(t, "p2_full")
                    guard(t, bname(pp))
                    for dc in range(DC):
                        ins = t.matmul(pp[:], wv_sl[par][:, dc * 128:(dc + 1) * 128],
                                 p1_fm[:, dc * 512:(dc + 1) * 512],
                                 start=(dc == 0), stop=(dc == DC - 1))
                    inc(ins, "T", 1)
                    reads(f"wv_sl{par}", "T", C["T"])
                    if d2c == DC - 1:
                        reads("p1_full", "T", C["T"])
                mm = C["T"]

                @block.vector
                def _(v):
                    w(v, "T", mm)
                    inc(v.tensor_copy(p2_fm[:, d2c * 512:(d2c + 1) * 512], pp[:]), "V", 1)
                    reads(bname(pp), "V", C["V"])
                p2_ev[d2c] = C["V"]

            if debug and tb == 0:
                @block.sync
                def _(y):
                    y.wait_ge(H["V"], p1_ev[DC - 1])
                    inc(y.dma_start(dbg["p1_0"].rearrange("c p n -> p c n"),
                                    p1_fm[:].rearrange("p (c n) -> p c n", n=512)), "st_dbg", 16)
                    reads("p1_full", "st_dbg", C["st_dbg"])
                    y.wait_ge(H["V"], p2_ev[DC - 1])
                    inc(y.dma_start(dbg["p2_0"].rearrange("c p n -> p c n"),
                                    p2_fm[:].rearrange("p (c n) -> p c n", n=512)), "st_dbg", 16)
                    reads("p2_full", "st_dbg", C["st_dbg"])

            woT_bf = sbuf(f"woT_bf_{tb}", [128, DC * D], BF16, es5b)
            ctx_tb = sbuf(f"ctx_tb_{tb}", [128, DC * 512], BF16, es5b)
            ctxp = sbuf(f"ctxp_{tb}", [128, DC * 512], BF16, es5b)
            tmp5 = sbuf(f"tmp5_{tb}", [128, 512], BF16, es5b)
            ao_bf = sbuf(f"ao_bf_{tb}", [128, D], BF16, es5b)

            @block.gpsimd
            def _(g):
                guard(g, "woT_bf")
                for ic in range(DC):
                    inc(g.dma_start(woT_bf[:, ic * D:(ic + 1) * D], B_wo[ic]), "ld_woT", 16)
                guard(g, "ctx_tb")
                inc(g.dma_start(ctx_tb[:].rearrange("p (c n) -> p c n", n=512),
                                S_ctx[tb].rearrange("c p n -> p c n")), "ld_ctxtb", 16)
            wo_ld = C["ld_woT"]
            ctx_ld = C["ld_ctxtb"]

            if tb == 0:
                @block.vector
                def _(v):
                    w(v, "ld_woT", wo_ld)
                    guard(v, "ao_bf")
                    for ic in range(DC):
                        vchain(v, v.tensor_mul(ao_bf[:], woT_bf[:, ic * D:(ic + 1) * D],
                                               gw_bf[:]))
                        vchain(v, v.reduce_sum(gv_sb[:, ic:ic + 1], ao_bf[:], axis=AX))
                    inc(v.tensor_copy(gv_bf[:], gv_sb[:]), "V", 1)
                    reads("ao_bf", "V", C["V"])
                gv_state["done"] = C["V"]
            gv_done = gv_state["done"]

            for tsub in range(4):
                pe = psEF[tsub % 2]

                @block.tensor
                def _(t):
                    if tsub == 0:
                        w(t, "V", gv_done)
                        w(t, "ld_ctxtb", ctx_ld)
                    guard(t, bname(pe))
                    for ic in range(DC):
                        ins = t.matmul(pe[0:1, 0:128], gv_bf[:, ic:ic + 1],
                                       ctx_tb[:, ic * 512 + tsub * 128: ic * 512 + tsub * 128 + 128],
                                       start=(ic == 0), stop=(ic == DC - 1))
                    inc(ins, "T", 1)
                bmm = C["T"]

                @block.vector
                def _(v):
                    w(v, "T", bmm)
                    if tsub == 0:
                        guard(v, "bp_row")
                    inc(v.tensor_copy(bp_row[0:1, tsub * 128:(tsub + 1) * 128], pe[0:1, 0:128]),
                        "V", 1)
                    reads(bname(pe), "V", C["V"])
            bp_done = C["V"]

            @block.vector
            def _(v):
                w(v, "V", bp_done)
                w(v, "V", rsr_ev)
                vchain(v, v.tensor_tensor(bp_row[0:1, :], bp_row[0:1, :],
                                          gb_sb[0:1, 0:1].to_broadcast((1, 512)), op=ALU.add))
                inc(v.reciprocal(rs_row[0:1, :], rs_row[0:1, :]), "V", 1)
            sg0 = C["V"]

            @block.scalar
            def _(s):
                w(s, "V", sg0)
                guard(s, "scl_row")
                inc(s.activation(scl_row[0:1, :], bp_row[0:1, :], ACT.Exp, scale=-1.0), "C", 1)
                reads("bp_row", "C", C["C"])
            sg1 = C["C"]

            @block.vector
            def _(v):
                w(v, "C", sg1)
                vchain(v, v.tensor_scalar_add(scl_row[0:1, :], scl_row[0:1, :], 1.0))
                vchain(v, v.reciprocal(scl_row[0:1, :], scl_row[0:1, :]))
                vchain(v, v.tensor_tensor(scl_row[0:1, :], scl_row[0:1, :], rs_row[0:1, :],
                                          op=ALU.mult))
                inc(v.tensor_copy(sclb_row[0:1, :], scl_row[0:1, :]), "V", 1)
                reads("scl_row", "V", C["V"])
                reads("rs_row", "V", C["V"])
            sg2 = C["V"]

            @block.tensor
            def _(t):
                w(t, "V", sg2)
                w(t, "V", ones_done)
                guard(t, bname(psE))
                ins = t.matmul(psE[:, :512], onesr_bf[:], sclb_row[0:1, :],
                               start=True, stop=True)
                inc(ins, "T", 1)
                reads("sclb_row", "T", C["T"])
            sbc = C["T"]

            @block.vector
            def _(v):
                w(v, "T", sbc)
                w(v, "V", p2_ev[DC - 1])
                guard(v, "ctxp")
                guard(v, "tmp5")
                for ic in range(DC):
                    v.tensor_tensor(tmp5[:], p2_fm[:, ic * 512:(ic + 1) * 512], psE[:, :512],
                                    op=ALU.mult)
                    ins = v.tensor_tensor(ctxp[:, ic * 512:(ic + 1) * 512], tmp5[:],
                                          ctx_tb[:, ic * 512:(ic + 1) * 512], op=ALU.add)
                inc(ins, "V", 1)
                reads(bname(psE), "V", C["V"])
                reads("tmp5", "V", C["V"])
                reads("p2_full", "V", C["V"])
                reads("ctx_tb", "V", C["V"])
            ctxp_done = C["V"]

            for tsub in range(4):
                for ob in range(4):
                    pp = psAB[ob % 2]

                    @block.tensor
                    def _(t):
                        if tsub == 0 and ob == 0:
                            w(t, "ld_woT", wo_ld)
                            w(t, "V", ctxp_done)
                        guard(t, bname(pp))
                        for ic in range(DC):
                            ins = t.matmul(pp[:],
                                           ctxp[:, ic * 512 + tsub * 128: ic * 512 + tsub * 128 + 128],
                                           woT_bf[:, ic * D + ob * 512: ic * D + (ob + 1) * 512],
                                           start=(ic == 0), stop=(ic == DC - 1))
                        inc(ins, "T", 1)
                        if tsub == 3 and ob == 3:
                            reads("ctxp", "T", C["T"])
                            reads("woT_bf", "T", C["T"])
                    mm = C["T"]

                    @block.vector
                    def _(v):
                        w(v, "T", mm)
                        if ob == 0:
                            guard(v, "ao_bf")
                        inc(v.tensor_copy(ao_bf[:, ob * 512:(ob + 1) * 512], pp[:]), "V", 1)
                        reads(bname(pp), "V", C["V"])
                    comb3 = C["V"]

                @block.sync
                def _(y):
                    y.wait_ge(H["V"], comb3)
                    inc(y.dma_start(S_ao[tb, tsub], ao_bf[:]), "st_ao", 16)
                    reads("ao_bf", "st_ao", C["st_ao"])

            close_stack(es5b)
            close_stack(es5o)
            tb_snap["w"] = [("T", C["T"]), ("V", C["V"]), ("st_ao", C["st_ao"])]

        barrier()
        if upto <= 3:
            _finish(block, H, C)
            close_all()
            return nc

        es67 = new_stack()
        y2_fm = sbuf("y2_fm", [128, DC * T], BF16)
        W1_bf = sbuf("W1_bf", [128, DC * D], BF16, es67)
        h_bf = sbuf("h_bf", [128, DC * 512], BF16, es67)
        lg8 = sbuf("lg8", [128, EXP], F32, es67)
        ee8 = sbuf("ee8", [128, EXP], F32, es67)
        nmx = sbuf("nmx", [128, 2], F32, es67)
        xt6 = [sbuf(f"xt6{i}", [128, D], F32, es67) for i in range(2)]
        ao6 = [sbuf(f"ao6{i}", [128, D], BF16, es67) for i in range(2)]
        x1t = sbuf("x1t", [128, D], F32, es67)
        x1b = sbuf("x1b", [128, D], BF16, es67)
        y2b = [sbuf(f"y2b{i}", [128, D], BF16, es67) for i in range(2)]

        @block.gpsimd
        def _(g):
            for dc in range(DC):
                inc(g.dma_start(W1_bf[:, dc * D:(dc + 1) * D],
                                W1_nat[dc * 128:(dc + 1) * 128, :]), "ld_w1", 16)
        w1_ld = C["ld_w1"]

        def emit_ln2(tc):
            par = tc % 2
            tb, tsub = tc // 4, tc % 4

            @block.gpsimd
            def _(g):
                guard(g, f"xt6{par}")
                guard(g, f"ao6{par}")
                inc(g.dma_start(xt6[par][:], x_in[HALO + tc * 128: HALO + (tc + 1) * 128, :]),
                    f"ld_p6{par}", 16)
                inc(g.dma_start(ao6[par][:], S_ao[tb, tsub]), f"ld_p6{par}", 16)
            x6_ld = C[f"ld_p6{par}"]

            @block.vector
            def _(v):
                w(v, f"ld_p6{par}", x6_ld)
                guard(v, "x1t")
                guard(v, "x1b")
                v.tensor_tensor(x1t[:], xt6[par][:], ao6[par][:], op=ALU.add)
                v.tensor_copy(x1b[:], x1t[:])
                vchain(v, v.reduce_sum(stat[:, 0:1], x1t[:], axis=AX))
                inc(v.tensor_scalar_mul(stat[:, 1:2], stat[:, 0:1], 1.0 / D), "V", 1)
                reads(f"xt6{par}", "V", C["V"])
                reads(f"ao6{par}", "V", C["V"])
            m6 = C["V"]

            @block.scalar
            def _(s):
                w(s, "V", m6)
                guard(s, "st23")
                inc(s.activation(xt6[par][:], x1t[:], ACT.Square, accum_out=stat[:, 2:3]), "C", 1)
                reads(f"xt6{par}", "C", C["C"])
            sq6 = C["C"]

            @block.vector
            def _(v):
                w(v, "C", sq6)
                w(v, "V", m6)
                vchain(v, v.tensor_scalar(stat[:, 4:5], stat[:, 1:2], stat[:, 1:2], EPS,
                                          op0=ALU.mult, op1=ALU.subtract))
                inc(v.tensor_scalar(stat[:, 5:6], stat[:, 2:3], 1.0 / D, stat[:, 4:5],
                                    op0=ALU.mult, op1=ALU.subtract), "V", 1)
                reads("st23", "V", C["V"])
            v6 = C["V"]

            @block.scalar
            def _(s):
                w(s, "V", v6)
                guard(s, "st67")
                inc(s.activation(stat[:, 6:7], stat[:, 5:6], ACT.Sqrt), "C", 1)
            sd6 = C["C"]

            @block.vector
            def _(v):
                w(v, "C", sd6)
                guard(v, f"y2b{par}")
                vchain(v, v.reciprocal(stat[:, 7:8], stat[:, 6:7]))
                ins = v.tensor_scalar(y2b[par][:], x1t[:], stat[:, 1:2], stat[:, 7:8],
                                      op0=ALU.subtract, op1=ALU.mult)
                inc(ins, "V", 1)
                reads("x1t", "V", C["V"])
                reads("st67", "V", C["V"])
            n6 = C["V"]

            @block.sync
            def _(y):
                y.wait_ge(H["V"], m6)
                inc(y.dma_start(S_x1[tc], x1b[:]), "st_x1", 16)
                reads("x1b", "st_x1", C["st_x1"])

            items = [(y2b[par][:, dc * 128:(dc + 1) * 128],
                      y2_fm[:, dc * T + tc * 128: dc * T + tc * 128 + 128])
                     for dc in range(DC)]
            r6 = tr_batch(items, [psT0, psT1], pre=[("V", n6)])
            reads(f"y2b{par}", "T", r6["lastT"])

        def emit_h_block(tb, y2v):
            h_ev = {}
            for ec in range(DC):
                pp = psAB[ec % 2]

                @block.tensor
                def _(t):
                    if ec == 0:
                        w(t, "V", y2v)
                        if tb == 0:
                            w(t, "ld_w1", w1_ld)
                    guard(t, bname(pp))
                    guard(t, f"h{ec}")
                    for dc in range(DC):
                        ins = t.matmul(pp[:],
                                       W1_bf[:, dc * D + ec * 128: dc * D + ec * 128 + 128],
                                       y2_fm[:, dc * T + tb * TBS: dc * T + (tb + 1) * TBS],
                                       start=(dc == 0), stop=(dc == DC - 1))
                    inc(ins, "T", 1)
                mm = C["T"]

                @block.vector
                def _(v):
                    w(v, "T", mm)
                    ins = v.tensor_scalar(h_bf[:, ec * 512:(ec + 1) * 512], pp[:],
                                          b1_sb[:, ec:ec + 1], 0.0,
                                          op0=ALU.add, op1=ALU.max)
                    inc(ins, "V", 1)
                    reads(bname(pp), "V", C["V"])
                h_ev[ec] = C["V"]
            return h_ev

        def emit_logits(tb, h_ev):
            for tsub in range(4):
                tc = tb * 4 + tsub
                pp = psEF[tsub % 2]

                @block.tensor
                def _(t):
                    w(t, "V", h_ev[DC - 1])
                    guard(t, bname(pp))
                    for ec in range(DC):
                        ins = t.matmul(pp[:, :EXP],
                                       h_bf[:, ec * 512 + tsub * 128: ec * 512 + tsub * 128 + 128],
                                       W2_bf[:, ec * EXP:(ec + 1) * EXP],
                                       start=(ec == 0), stop=(ec == DC - 1))
                    inc(ins, "T", 1)
                    if tsub == 3:
                        for ec in range(DC):
                            reads(f"h{ec}", "T", C["T"])
                lgm = C["T"]

                @block.vector
                def _(v):
                    w(v, "T", lgm)
                    guard(v, "lg8")
                    guard(v, "nmx0")
                    vchain(v, v.tensor_tensor(lg8[:], pp[:, :EXP], b2_sb[:], op=ALU.add))
                    inc(v.reduce_max(nmx[:, 0:1], lg8[:], axis=AX, negate=True), "V", 1)
                    reads(bname(pp), "V", C["V"])
                lgv = C["V"]

                @block.scalar
                def _(s):
                    w(s, "V", lgv)
                    guard(s, "ee8")
                    inc(s.activation(ee8[:], lg8[:], ACT.Exp, bias=nmx[:, 0:1]), "C", 1)
                    reads("lg8", "C", C["C"])
                    reads("nmx0", "C", C["C"])
                eev = C["C"]

                @block.vector
                def _(v):
                    w(v, "C", eev)
                    vchain(v, v.reduce_sum(nmx[:, 1:2], ee8[:], axis=AX))
                    vchain(v, v.reciprocal(nmx[:, 1:2], nmx[:, 1:2]))
                    ins = v.tensor_scalar_mul(alpha_sb[:, tc * EXP:(tc + 1) * EXP],
                                              ee8[:], nmx[:, 1:2])
                    inc(ins, "V", 1)
                    reads("ee8", "V", C["V"])

        prev = None
        for tb in range(NTB):
            for tsub in range(4):
                emit_ln2(tb * 4 + tsub)
            y2v = C["V"]
            if prev is not None:
                emit_logits(prev[0], prev[1])
            h_ev = emit_h_block(tb, y2v)
            prev = (tb, h_ev)
        emit_logits(prev[0], prev[1])

        close_stack(es67)
        if debug:
            @block.sync
            def _(y):
                y.wait_ge(H["V"], C["V"])
                inc(y.dma_start(dbg["y2fm"].rearrange("c p n -> p c n"),
                                y2_fm[:].rearrange("p (c n) -> p c n", n=T)), "st_dbg", 16)
                inc(y.dma_start(dbg["alpha"], alpha_sb[:]), "st_dbg", 16)
                inc(y.dma_start(dbg["rs"], rowsums[:]), "st_dbg", 16)
        if upto <= 5:
            _finish(block, H, C)
            close_all()
            return nc
        barrier()

        es8 = new_stack()
        VT_bf = sbuf("VT_bf", [128, DC * 512], BF16, es8)
        UT_bf = sbuf("UT_bf", [128, 4 * D], BF16, es8)
        avx_bf = sbuf("avx_bf", [128, 512], BF16, es8)
        avx_fm = sbuf("avx_fm", [128, 512], BF16, es8)
        x1in = [sbuf(f"x1in{i}", [128, D], BF16, es8) for i in range(2)]
        outt = [sbuf(f"outt{i}", [128, D], F32, es8) for i in range(2)]

        @block.gpsimd
        def _(g):
            for dc in range(DC):
                inc(g.dma_start(VT_bf[:, dc * 512:(dc + 1) * 512],
                                VT_in[dc * 128:(dc + 1) * 128, :]), "ld_vut", 16)
            for krc in range(4):
                inc(g.dma_start(UT_bf[:, krc * D:(krc + 1) * D],
                                UT_in[krc * 128:(krc + 1) * 128, :]), "ld_vut", 16)
        vu_ld = C["ld_vut"]

        for tc in range(NTC):
            par = tc % 2
            pv = psAB[tc % 2]

            @block.gpsimd
            def _(g):
                guard(g, f"x1in{par}")
                inc(g.dma_start(x1in[par][:], S_x1[tc]), f"ld_x1in{par}", 16)
            x1_ld = C[f"ld_x1in{par}"]

            @block.tensor
            def _(t):
                if tc == 0:
                    w(t, "ld_vut", vu_ld)
                guard(t, bname(pv))
                for dc in range(DC):
                    ins = t.matmul(pv[:], y2_fm[:, dc * T + tc * 128: dc * T + tc * 128 + 128],
                             VT_bf[:, dc * 512:(dc + 1) * 512],
                             start=(dc == 0), stop=(dc == DC - 1))
                inc(ins, "T", 1)
            vxm = C["T"]

            @block.vector
            def _(v):
                w(v, "T", vxm)
                guard(v, "avx_bf")
                for k in range(EXP):
                    ins = v.tensor_scalar_mul(avx_bf[:, k * R:(k + 1) * R],
                                              pv[:, k * R:(k + 1) * R],
                                              alpha_sb[:, tc * EXP + k: tc * EXP + k + 1])
                inc(ins, "V", 1)
                reads(bname(pv), "V", C["V"])
            avxd = C["V"]

            items = [(avx_bf[:, c * 128:(c + 1) * 128],
                      avx_fm[:, c * 128:(c + 1) * 128]) for c in range(4)]
            r8 = tr_batch(items, [psT0, psT1],
                          pre=[("V", avxd)], dst_guards=["avx_fm"])
            reads("avx_bf", "T", r8["lastT"])
            avcp8 = dict(r8["done"])

            out_done = {}
            for db in range(4):
                pp = psCD[db % 2]

                @block.tensor
                def _(t):
                    for s_, v_ in avcp8.items():
                        w(t, s_, v_)
                    guard(t, bname(pp))
                    for krc in range(4):
                        ins = t.matmul(pp[:], avx_fm[:, krc * 128:(krc + 1) * 128],
                                 UT_bf[:, krc * D + db * 512: krc * D + (db + 1) * 512],
                                 start=(krc == 0), stop=(krc == 3))
                    inc(ins, "T", 1)
                    if db == 3:
                        reads("avx_fm", "T", C["T"])
                ofm = C["T"]

                @block.vector
                def _(v):
                    w(v, "T", ofm)
                    w(v, f"ld_x1in{par}", x1_ld)
                    guard(v, f"outt{par}_{db}")
                    ins = v.tensor_tensor(outt[par][:, db * 512:(db + 1) * 512], pp[:],
                                          x1in[par][:, db * 512:(db + 1) * 512], op=ALU.add)
                    inc(ins, "V", 1)
                    reads(bname(pp), "V", C["V"])
                    if db == 3:
                        reads(f"x1in{par}", "V", C["V"])
                out_done[db] = C["V"]

            @block.sync
            def _(y):
                y.wait_ge(H["V"], out_done[3])
                inc(y.dma_start(x_out[tc * 128:(tc + 1) * 128, :], outt[par][:]), "st_out", 16)
                for db in range(4):
                    reads(f"outt{par}_{db}", "st_out", C["st_out"])

        close_stack(es8)
        _finish(block, H, C)
        close_all()
    return nc


def _dump_yfm(block, H, C, inc, y_fm, dbg):
    if "yfm" not in dbg:
        return

    @block.sync
    def _(y):
        y.wait_ge(H["V"], C["V"])
        y.wait_ge(H["C"], C["C"])
        inc(y.dma_start(dbg["yfm"].rearrange("c p n -> p c n"),
                        y_fm[:].rearrange("p (c n) -> p c n", n=TH)), "st_dbg", 16)


def _finish(block, H, C):
    @block.sync
    def _(y):
        for s, v in C.items():
            if v > 0:
                y.wait_ge(H[s], v)


def _prep_inputs(inputs):
    f = np.float32
    hs = np.asarray(inputs["L0_attn_hidden_states"], dtype=f)
    wq = np.asarray(inputs["L0_attn_wq"], dtype=f)
    wk = np.asarray(inputs["L0_attn_wk"], dtype=f)
    wv = np.asarray(inputs["L0_attn_wv"], dtype=f)
    wo = np.asarray(inputs["L0_attn_wo"], dtype=f)
    gate_w = np.asarray(inputs["L0_attn_gate_w"], dtype=f)
    gate_b = np.asarray(inputs["L0_attn_gate_b"], dtype=f)
    mem_bank = np.asarray(inputs["L0_attn_mem_bank"], dtype=f)
    U = np.asarray(inputs["L0_ffn_U"], dtype=f)
    V = np.asarray(inputs["L0_ffn_V"], dtype=f)
    W1 = np.asarray(inputs["L0_ffn_W1"], dtype=f)
    b1 = np.asarray(inputs["L0_ffn_b1"], dtype=f)
    W2 = np.asarray(inputs["L0_ffn_W2"], dtype=f)
    b2 = np.asarray(inputs["L0_ffn_b2"], dtype=f)

    def blk_T(wm):
        return np.ascontiguousarray(wm.T.reshape(D, NH, HD).transpose(1, 0, 2))

    shared = {
        "wqT_h": blk_T(wq), "wkT_h": blk_T(wk), "wvT_h": blk_T(wv),
        "wk_nat": np.ascontiguousarray(wk),
        "woT": np.ascontiguousarray(wo.T),
        "mbT_pb": np.ascontiguousarray(
            mem_bank.T.reshape(DC, 128, 8, 512).transpose(2, 1, 0, 3)),
        "mb_pb": np.ascontiguousarray(
            mem_bank.reshape(32, 128, DC, 128).transpose(2, 1, 0, 3)),
        "wvT_pb": np.ascontiguousarray(
            wv.T.reshape(DC, 128, DC, 128).transpose(2, 1, 0, 3)),
        "W1_nat": np.ascontiguousarray(W1),
        "b1c": np.ascontiguousarray(b1.reshape(DC, 128).T),
        "W2r": np.ascontiguousarray(W2.reshape(DC, 128, EXP)),
        "b2b": np.ascontiguousarray(np.broadcast_to(b2[None, :], (128, EXP))),
        "gw_rep": np.ascontiguousarray(np.broadcast_to(gate_w.reshape(1, D), (128, D))),
        "gb_rep": np.full((128, 1), float(np.asarray(gate_b).reshape(-1)[0]), dtype=f),
        "VT_in": np.ascontiguousarray(V.transpose(2, 0, 1).reshape(D, EXP * R)),
        "UT_in": np.ascontiguousarray(U.transpose(0, 2, 1).reshape(EXP * R, D)),
        "iden_in": np.eye(128, dtype=f),
    }

    r = np.arange(128)[:, None]
    c = np.arange(384)[None, :]
    m_main = ((c > r) & (c <= r + WIN)).astype(f)
    mask_even = np.stack([m_main * (c >= 256), m_main * (c >= 128), m_main])
    mask_odd = np.stack([m_main, m_main, m_main])

    in_maps = []
    Tq = hs.shape[1]
    for core in range(8):
        b, half = core // 2, core % 2
        if half == 0:
            xc = np.concatenate([np.zeros((HALO, D), dtype=f), hs[b, :T]], axis=0)
            mk = mask_even
        else:
            xc = hs[b, T - HALO: Tq]
            mk = mask_odd
        m = dict(shared)
        m["x_in"] = np.ascontiguousarray(xc)
        m["mask3"] = np.ascontiguousarray(mk)
        in_maps.append(m)
    return in_maps, hs.shape


def kernel(**inputs):
    in_maps, full_shape = _prep_inputs(inputs)
    nc = build_program(debug=False)
    res = run_bass_kernel_spmd(nc, in_maps, core_ids=list(range(8)))
    B, Tq, _ = full_shape
    out = np.empty((B, Tq, D), dtype=np.float32)
    for core in range(8):
        b, half = core // 2, core % 2
        out[b, half * T:(half + 1) * T] = res.results[core]["x_out"]
    return out


# version 16
# speedup vs baseline: 1.0052x; 1.0052x over previous
import numpy as np
import concourse.bass as bass
import concourse.mybir as mybir
from concourse.bass_utils import run_bass_kernel_spmd
from contextlib import ExitStack

F32 = mybir.dt.float32
BF16 = mybir.dt.bfloat16
AX = mybir.AxisListType.X
ALU = mybir.AluOpType
ACT = mybir.ActivationFunctionType

D = 2048
T = 2048
HALO = 256
TH = T + HALO
NH = 16
HD = 128
WIN = 256
M = 4096
EXP = 8
R = 64
DC = D // 128
TBS = 512
NTB = T // TBS
NTC = T // 128
NTHC = TH // 128
ISQD = float(1.0 / np.sqrt(D))
ISQH = float(1.0 / np.sqrt(HD))
EPS = 1e-5
P0_TILES = NTHC
HK_JCS = DC


def build_program(debug=False, upto=99):
    nc = bass.Bass("TRN2", target_bir_lowering=False)
    dt_dbg = "ExternalOutput" if debug else "Internal"

    def din(name, shape):
        return nc.dram_tensor(name, shape, F32, kind="ExternalInput").ap()

    x_in = din("x_in", [TH, D])
    wqT_h = din("wqT_h", [NH, D, HD])
    wkT_h = din("wkT_h", [NH, D, HD])
    wvT_h = din("wvT_h", [NH, D, HD])
    wk_nat = din("wk_nat", [D, D])
    woT = din("woT", [D, D])
    mbT_pb = din("mbT_pb", [8, 128, DC, 512])
    mb_pb = din("mb_pb", [DC, 128, 32, 128])
    wvT_pb = din("wvT_pb", [DC, 128, DC, 128])
    W1_nat = din("W1_nat", [D, D])
    b1c = din("b1c", [128, DC])
    W2r = din("W2r", [DC, 128, EXP])
    b2b = din("b2b", [128, EXP])
    gw_rep = din("gw_rep", [128, D])
    gb_rep = din("gb_rep", [128, 1])
    VT_in = din("VT_in", [D, EXP * R])
    UT_in = din("UT_in", [EXP * R, D])
    iden_in = din("iden_in", [128, 128])
    mask3 = din("mask3", [3, 128, 384])

    x_out = nc.dram_tensor("x_out", [T, D], F32, kind="ExternalOutput").ap()

    S_hk = nc.dram_tensor("S_hk", [NTB, DC, 128, TBS], BF16, kind=dt_dbg).ap()
    B_mbT = nc.dram_tensor("B_mbT", [8, 128, DC, 512], BF16, kind="Internal").ap()
    B_mb = nc.dram_tensor("B_mb", [DC, 128, 32, 128], BF16, kind="Internal").ap()
    B_wv = nc.dram_tensor("B_wv", [DC, 128, DC, 128], BF16, kind="Internal").ap()
    B_wo = nc.dram_tensor("B_wo", [DC, 128, D], BF16, kind="Internal").ap()
    S_ctx = nc.dram_tensor("S_ctx", [NTB, NH, 128, 512], BF16, kind=dt_dbg).ap()
    S_ao = nc.dram_tensor("S_ao", [NTB, 4, 128, D], BF16, kind=dt_dbg).ap()
    S_x1 = nc.dram_tensor("S_x1", [NTC, 128, D], BF16, kind=dt_dbg).ap()
    dbg = {}
    if debug:
        dbg["yfm"] = nc.dram_tensor("D_yfm", [DC, 128, TH], BF16, kind="ExternalOutput").ap()
        dbg["sim0"] = nc.dram_tensor("D_sim0", [4, 128, M], F32, kind="ExternalOutput").ap()
        dbg["p1_0"] = nc.dram_tensor("D_p1_0", [DC, 128, TBS], BF16, kind="ExternalOutput").ap()
        dbg["p2_0"] = nc.dram_tensor("D_p2_0", [DC, 128, TBS], BF16, kind="ExternalOutput").ap()
        dbg["rs"] = nc.dram_tensor("D_rs", [128, NTC], F32, kind="ExternalOutput").ap()
        dbg["alpha"] = nc.dram_tensor("D_alpha", [128, NTC * EXP], F32, kind="ExternalOutput").ap()
        dbg["y2fm"] = nc.dram_tensor("D_y2fm", [DC, 128, T], BF16, kind="ExternalOutput").ap()

    es = ExitStack()
    open_stacks = []

    def new_stack():
        st = ExitStack()
        open_stacks.append(st)
        return st

    def close_stack(st):
        assert open_stacks and open_stacks[-1] is st, "non-LIFO stack close"
        open_stacks.pop()
        st.close()

    def close_all():
        while open_stacks:
            open_stacks.pop().close()

    with es:
        def sbuf(name, shape, dt, stack=None):
            return (stack or es).enter_context(nc.sbuf_tensor(name, shape, dt))

        iden_bf = sbuf("iden_bf", [128, 128], BF16)
        masks_bf = sbuf("masks_bf", [128, 3 * 384], BF16)
        b1_sb = sbuf("b1_sb", [128, DC], F32)
        W2_bf = sbuf("W2_bf", [128, DC * EXP], BF16)
        b2_sb = sbuf("b2_sb", [128, EXP], F32)
        gw_bf = sbuf("gw_bf", [128, D], BF16)
        gb_sb = sbuf("gb_sb", [128, 1], F32)
        rowsums = sbuf("rowsums", [128, NTC], F32)
        alpha_sb = sbuf("alpha_sb", [128, NTC * EXP], F32)
        stat = sbuf("stat", [128, 16], F32)
        top8 = sbuf("top8", [128, 8], F32)
        ones_bf = sbuf("ones_bf", [128, 1], BF16)
        onesr_bf = sbuf("onesr_bf", [1, 128], BF16)
        gv_sb = sbuf("gv_sb", [128, DC], F32)
        gv_bf = sbuf("gv_bf", [128, DC], BF16)
        bp_row = sbuf("bp_row", [1, 512], F32)
        rs_row = sbuf("rs_row", [1, 512], F32)
        scl_row = sbuf("scl_row", [1, 512], F32)
        sclb_row = sbuf("sclb_row", [1, 512], BF16)

        psA = es.enter_context(nc.psum_tensor("psA", [128, 512], F32))
        psB = es.enter_context(nc.psum_tensor("psB", [128, 512], F32))
        psC = es.enter_context(nc.psum_tensor("psC", [128, 512], F32))
        psD = es.enter_context(nc.psum_tensor("psD", [128, 512], F32))
        psE = es.enter_context(nc.psum_tensor("psE", [128, 512], F32))
        psF = es.enter_context(nc.psum_tensor("psF", [128, 512], F32))
        psT0 = es.enter_context(nc.psum_tensor("psT0", [128, 512], F32))
        psT1 = es.enter_context(nc.psum_tensor("psT1", [128, 512], F32))
        psAB = [psA, psB]
        psCD = [psC, psD]
        psEF = [psE, psF]
        psT = [psT0, psT1]
        PSNAME = {id(psA): "psA", id(psB): "psB", id(psC): "psC", id(psD): "psD",
                  id(psE): "psE", id(psF): "psF", id(psT0): "psT0", id(psT1): "psT1"}

        gv_state = {"done": 0}
        H = {s: es.enter_context(nc.semaphore(s)) for s in ("T", "V", "C")}
        C = {s: 0 for s in H}
        ENG_SEM = {"tensor": "T", "vector": "V", "scalar": "C", "gpsimd": None, "sync": None}
        block = es.enter_context(nc.Block())

        def sem_of(s):
            if s not in H:
                H[s] = es.enter_context(nc.semaphore(f"d_{s}"))
                C[s] = 0
            return H[s]

        def inc(instr, s, n):
            instr.then_inc(sem_of(s), n)
            C[s] += n
            return C[s]

        def w(eng, s, v):
            if v > 0:
                eng.wait_ge(sem_of(s), v)

        def vchain(v, ins):
            c = inc(ins, "V", 1)
            v.wait_ge(H["V"], c)

        def tr_batch(items, banks, evac="vector", pre=None, dst_guards=None):
            gd = {}
            for k in (dst_guards or []):
                for s_, v_ in (readers.pop(k, None) or {}).items():
                    gd[s_] = max(gd.get(s_, 0), v_)
            eng_first = {"vector": True, "scalar": True}
            done = {}
            lastT = 0
            ncyc = (len(items) + 3) // 4
            for ci in range(ncyc):
                grp = items[ci * 4:(ci + 1) * 4]
                b = banks[ci % len(banks)]

                def tbody(t, grp=grp, b=b, first=(ci == 0)):
                    if first and pre:
                        for s_, v_ in pre:
                            w(t, s_, v_)
                    guard(t, bname(b))
                    for c, (srcap, dstap) in enumerate(grp):
                        ins = t.matmul(b[:, c * 128:(c + 1) * 128], srcap, iden_bf[:],
                                       start=True, stop=True)
                    inc(ins, "T", 1)
                block.tensor(tbody)
                lastT = C["T"]

                engname = ("vector" if (ci % 2 == 0) else "scalar") if evac == "alt" else evac
                sem = "V" if engname == "vector" else "C"

                def ebody(e, grp=grp, b=b, mm=lastT, engname=engname, sem=sem):
                    w(e, "T", mm)
                    if eng_first[engname]:
                        eng_first[engname] = False
                        for s_, v_ in gd.items():
                            w(e, s_, v_)
                    ins = None
                    for c, (srcap, dstap) in enumerate(grp):
                        if engname == "vector":
                            ins = e.tensor_copy(dstap, b[:, c * 128:(c + 1) * 128])
                        else:
                            ins = e.copy(dstap, b[:, c * 128:(c + 1) * 128])
                    inc(ins, sem, 1)
                    reads(bname(b), sem, C[sem])
                getattr(block, engname)(ebody)
                done[sem] = C[sem]
            return {"lastT": lastT, "done": done}

        readers = {}

        def reads(key, s, v):
            d = readers.setdefault(key, {})
            d[s] = max(d.get(s, 0), v)

        def guard(eng, key):
            d = readers.pop(key, None)
            if d:
                for s, v in d.items():
                    w(eng, s, v)

        def bname(ps):
            return PSNAME[id(ps)]

        def barrier():
            snap = dict(C)
            for engname in ("tensor", "vector", "scalar", "gpsimd", "sync"):
                own = ENG_SEM[engname]

                def body(e, snap=snap, own=own):
                    for s, v in snap.items():
                        if s != own and v > 0:
                            e.wait_ge(H[s], v)
                getattr(block, engname)(body)

        es_pc = new_stack()
        pcb = [sbuf(f"pcb{i}", [128, 4096], BF16, es_pc) for i in range(2)]
        pc_units = ([("mbT", (j, hh)) for j in range(8) for hh in range(2)] +
                    [("mb", j) for j in range(DC)] +
                    [("wv", j) for j in range(DC)] +
                    [("wo", j) for j in range(DC)])
        pc_state = {"i": 0, "s": 0}
        pc_ld = {}

        def pc_pump():
            i = pc_state["s"]
            if i < pc_state["i"]:
                kind, j = pc_units[i]
                par = i % 2
                nsz = {"mbT": 4096, "mb": 4096, "wv": 2048, "wo": 2048}[kind]
                if kind == "mbT":
                    jj, hh = j
                    dst = B_mbT[jj].rearrange("p c n -> p (c n)")[:, hh * 4096:(hh + 1) * 4096]
                else:
                    dst = {"mb": lambda: B_mb[j].rearrange("p c n -> p (c n)"),
                           "wv": lambda: B_wv[j].rearrange("p c n -> p (c n)"),
                           "wo": lambda: B_wo[j]}[kind]()

                def sbody(y, i=i, par=par, nsz=nsz, dst=dst):
                    w(y, f"ld_pcb{par}", pc_ld[i])
                    inc(y.dma_start(dst, pcb[par][:, :nsz]), f"st_pcb{par}", 16)
                    reads(f"pcb{par}", f"st_pcb{par}", C[f"st_pcb{par}"])
                block.sync(sbody)
                pc_state["s"] = i + 1
            i = pc_state["i"]
            if i < len(pc_units):
                kind, j = pc_units[i]
                par = i % 2
                if kind == "mbT":
                    jj, hh = j
                    src_ap = mbT_pb[jj].rearrange("p c n -> p (c n)")[:, hh * 4096:(hh + 1) * 4096]
                else:
                    src_ap = {"mb": lambda: mb_pb[j].rearrange("p c n -> p (c n)"),
                              "wv": lambda: wvT_pb[j].rearrange("p c n -> p (c n)"),
                              "wo": lambda: woT[j * 128:(j + 1) * 128, :]}[kind]()
                nsz = {"mbT": 4096, "mb": 4096, "wv": 2048, "wo": 2048}[kind]

                def lbody(g, i=i, par=par, src_ap=src_ap, nsz=nsz):
                    guard(g, f"pcb{par}")
                    pc_ld[i] = inc(g.dma_start(pcb[par][:, :nsz], src_ap), f"ld_pcb{par}", 16)
                block.gpsimd(lbody)
                pc_state["i"] = i + 1

        def pc_drain():
            while pc_state["s"] < len(pc_units):
                pc_pump()

        @block.gpsimd
        def _(g):
            inc(g.dma_start(iden_bf[:], iden_in), "consts", 16)
            inc(g.dma_start(masks_bf[:].rearrange("p (i n) -> p i n", n=384), mask3.rearrange("i p n -> p i n")), "consts", 16)
            inc(g.dma_start(b1_sb[:], b1c), "consts", 16)
            inc(g.dma_start(W2_bf[:].rearrange("p (c k) -> p c k", k=EXP), W2r.rearrange("c p k -> p c k")), "consts", 16)
            inc(g.dma_start(b2_sb[:], b2b), "consts", 16)
            inc(g.dma_start(gw_bf[:], gw_rep), "consts", 16)
            inc(g.dma_start(gb_sb[:], gb_rep), "consts", 16)
        consts_ld = C["consts"]

        @block.vector
        def _(v):
            v.memset(ones_bf[:], 1.0)
            inc(v.memset(onesr_bf[:], 1.0), "V", 1)
        ones_done = C["V"]

        es_y = new_stack()
        es0 = new_stack()
        y_fm = sbuf("y_fm", [128, DC * TH], BF16, es_y)
        xt = [sbuf(f"xt{i}", [128, D], F32, es0) for i in range(2)]
        junk = sbuf("junk", [128, D], F32, es0)
        ybf = [sbuf(f"ybf{i}", [128, D], BF16, es0) for i in range(2)]

        for thc in range(P0_TILES):
            par = thc % 2

            @block.gpsimd
            def _(g):
                guard(g, f"xt{par}")
                inc(g.dma_start(xt[par][:], x_in[thc * 128:(thc + 1) * 128, :]), f"ld_xt{par}", 16)
            x_ld = C[f"ld_xt{par}"]
            pc_pump()

            @block.vector
            def _(v):
                w(v, f"ld_xt{par}", x_ld)
                vchain(v, v.reduce_sum(stat[:, 0:1], xt[par][:], axis=AX))
                inc(v.tensor_scalar_mul(stat[:, 1:2], stat[:, 0:1], 1.0 / D), "V", 1)
            mean_done = C["V"]

            @block.scalar
            def _(s):
                w(s, f"ld_xt{par}", x_ld)
                guard(s, "st23")
                inc(s.activation(junk[:], xt[par][:], ACT.Square, accum_out=stat[:, 2:3]), "C", 1)
            sq_done = C["C"]

            @block.vector
            def _(v):
                w(v, "C", sq_done)
                w(v, "V", mean_done)
                vchain(v, v.tensor_scalar(stat[:, 4:5], stat[:, 1:2], stat[:, 1:2], EPS,
                                          op0=ALU.mult, op1=ALU.subtract))
                inc(v.tensor_scalar(stat[:, 5:6], stat[:, 2:3], 1.0 / D, stat[:, 4:5],
                                    op0=ALU.mult, op1=ALU.subtract), "V", 1)
                reads("st23", "V", C["V"])
            var_done = C["V"]

            @block.scalar
            def _(s):
                w(s, "V", var_done)
                guard(s, "st67")
                inc(s.activation(stat[:, 6:7], stat[:, 5:6], ACT.Sqrt), "C", 1)
            sd_done = C["C"]

            @block.vector
            def _(v):
                w(v, "C", sd_done)
                guard(v, f"ybf{par}")
                vchain(v, v.reciprocal(stat[:, 7:8], stat[:, 6:7]))
                ins = v.tensor_scalar(ybf[par][:], xt[par][:], stat[:, 1:2], stat[:, 7:8],
                                      op0=ALU.subtract, op1=ALU.mult)
                inc(ins, "V", 1)
                reads(f"xt{par}", "V", C["V"])
                reads("st67", "V", C["V"])
            norm_done = C["V"]

            pre0 = [("V", norm_done)]
            if thc == 0:
                pre0.append(("consts", consts_ld))
            items = [(ybf[par][:, dc * 128:(dc + 1) * 128],
                      y_fm[:, dc * TH + thc * 128: dc * TH + thc * 128 + 128])
                     for dc in range(DC)]
            r0 = tr_batch(items, [psT0, psT1, psE, psF], pre=pre0)
            reads(f"ybf{par}", "T", r0["lastT"])

        close_stack(es0)
        if upto <= 0:

            _dump_yfm(block, H, C, inc, y_fm, dbg)
            _finish(block, H, C)
            close_all()
            return nc
        barrier()

        es_hk = new_stack()
        wk_bf = sbuf("wk_bf", [128, DC * 1024], BF16, es_hk)
        hko = [sbuf(f"hko{i}", [128, 512], BF16, es_hk) for i in range(2)]

        def emit_wk_half(half):
            @block.gpsimd
            def _(g):
                guard(g, "wk_bf")
                for dc in range(DC):
                    inc(g.dma_start(wk_bf[:, dc * 1024:(dc + 1) * 1024],
                                    wk_nat[dc * 128:(dc + 1) * 128,
                                           half * 1024:(half + 1) * 1024]), "ld_wk", 16)
            return C["ld_wk"]

        wk_ld = [emit_wk_half(0), None]

        def emit_hk_group(it):
            jc = it // NTB
            tb = it % NTB
            jloc = jc % 8
            par = it % 2
            pp = psAB[par]

            @block.tensor
            def _(t):
                w(t, "ld_wk", wk_ld[jc // 8])
                guard(t, bname(pp))
                for dc in range(DC):
                    ins = t.matmul(pp[:],
                                   wk_bf[:, dc * 1024 + jloc * 128: dc * 1024 + jloc * 128 + 128],
                                   y_fm[:, dc * TH + HALO + tb * TBS: dc * TH + HALO + (tb + 1) * TBS],
                                   start=(dc == 0), stop=(dc == DC - 1))
                inc(ins, "T", 1)
                if it == 8 * NTB - 1:
                    reads("wk_bf", "T", C["T"])
            mm = C["T"]

            @block.vector
            def _(v):
                w(v, "T", mm)
                guard(v, f"hko{par}")
                inc(v.tensor_copy(hko[par][:], pp[:]), "V", 1)
                reads(bname(pp), "V", C["V"])
            ev = C["V"]

            @block.sync
            def _(y):
                y.wait_ge(H["V"], ev)
                inc(y.dma_start(S_hk[tb, jc], hko[par][:]), f"st_hko{par}", 16)
                reads(f"hko{par}", f"st_hko{par}", C[f"st_hko{par}"])

        hkq = []
        for it in range(8 * NTB):
            hkq.append(lambda it=it: emit_hk_group(it))
        hkq.append(lambda: wk_ld.__setitem__(1, emit_wk_half(1)))
        for it in range(8 * NTB, HK_JCS * NTB):
            hkq.append(lambda it=it: emit_hk_group(it))

        if upto <= 1:
            while hkq:
                hkq.pop(0)()
            close_stack(es_hk)
            _dump_yfm(block, H, C, inc, y_fm, dbg)
            _finish(block, H, C)
            close_all()
            return nc

        es2 = new_stack()
        wsl = {nm: [sbuf(f"wsl_{nm}{i}", [128, D], BF16, es2) for i in range(2)]
               for nm in ("q", "k", "v")}
        qf = [sbuf(f"qf{i}", [128, T], BF16, es2) for i in range(2)]
        kf = [sbuf(f"kf{i}", [128, TH], BF16, es2) for i in range(2)]
        vf = sbuf("vf", [128, TH], BF16, es2)
        vtm = [sbuf(f"vtm{i}", [128, TH], BF16, es2) for i in range(2)]
        p_e = [sbuf(f"p_e{i}", [128, 384], BF16, es2) for i in range(2)]
        p_m = [sbuf(f"p_m{i}", [128, 384], BF16, es2) for i in range(2)]
        at_bf = [sbuf(f"at_bf{i}", [128, 384], BF16, es2) for i in range(2)]
        ctxo = [sbuf(f"ctxo{i}", [128, 512], BF16, es2) for i in range(2)]
        rs1 = sbuf("rs1", [128, 4], F32, es2)

        wsl_ld = {}

        def head_srcs(h):
            return [(nm, src) for nm, src in zip(
                ("q", "k", "v"),
                [wqT_h[h].rearrange("(c p) o -> p c o", p=128),
                 wkT_h[h].rearrange("(c p) o -> p c o", p=128),
                 wvT_h[h].rearrange("(c p) o -> p c o", p=128)])]

        @block.gpsimd
        def _(g):
            for h in (0, 1):
                for nm, src in head_srcs(h):
                    guard(g, f"wsl_{nm}{h % 2}")
                    inc(g.dma_start(wsl[nm][h % 2][:].rearrange("p (c o) -> p c o", o=128), src),
                        f"ld_wsl{h % 2}", 16)
                wsl_ld[h] = C[f"ld_wsl{h % 2}"]

        qkv_ev = {}
        QKV_GROUPS = ([("q", tb) for tb in range(4)] +
                      [("k", tb) for tb in range(5)] +
                      [("v", tb) for tb in range(5)])
        WIDTHS = {"q": T, "k": TH, "v": TH}
        SRCOFF = {"q": HALO, "k": 0, "v": 0}

        def emit_qkv_group(h, gi):
            par = h % 2
            nm, tb = QKV_GROUPS[gi]
            dst = {"q": qf[par], "k": kf[par], "v": vf}[nm]
            pp = psEF[gi % 2]
            n = min(512, WIDTHS[nm] - tb * 512)

            @block.tensor
            def _(t):
                if gi == 0:
                    w(t, f"ld_wsl{par}", wsl_ld[h])
                    guard(t, f"qf{par}")
                    guard(t, f"kf{par}")
                if nm == "v" and tb == 0:
                    guard(t, "vf")
                guard(t, bname(pp))
                for dc in range(DC):
                    ins = t.matmul(pp[:, :n], wsl[nm][par][:, dc * 128:(dc + 1) * 128],
                                   y_fm[:, dc * TH + SRCOFF[nm] + tb * 512:
                                        dc * TH + SRCOFF[nm] + tb * 512 + n],
                                   start=(dc == 0), stop=(dc == DC - 1))
                inc(ins, "T", 1)
                if gi == len(QKV_GROUPS) - 1:
                    reads(f"wsl_q{par}", "T", C["T"])
                    reads(f"wsl_k{par}", "T", C["T"])
                    reads(f"wsl_v{par}", "T", C["T"])
            mm = C["T"]

            @block.vector
            def _(v):
                w(v, "T", mm)
                inc(v.tensor_copy(dst[:, tb * 512: tb * 512 + n], pp[:, :n]), "V", 1)
                reads(bname(pp), "V", C["V"])
            qkv_ev[(h, nm, tb)] = ("V", C["V"])

        def emit_attn_pair(h, qt0, fillers=()):
            par = h % 2
            qt1 = qt0 + 1
            fillers = list(fillers)

            @block.tensor
            def _(t):
                w(t, *qkv_ev[(h, "q", (qt1 * 128) // 512)])
                w(t, *qkv_ev[(h, "k", min((qt1 * 128 + 383) // 512, 4))])
                guard(t, bname(psA))
                t.matmul(psA[:, :384], qf[par][:, qt0 * 128:(qt0 + 1) * 128],
                         kf[par][:, qt0 * 128: qt0 * 128 + 384], start=True, stop=True)
                guard(t, bname(psB))
                ins = t.matmul(psB[:, :384], qf[par][:, qt1 * 128:(qt1 + 1) * 128],
                               kf[par][:, qt1 * 128: qt1 * 128 + 384], start=True, stop=True)
                inc(ins, "T", 1)
                if qt1 == NTC - 1:
                    reads(f"qf{par}", "T", C["T"])
                    reads(f"kf{par}", "T", C["T"])
            sc = C["T"]
            if fillers and fillers[0][0] == "qkv":
                fillers.pop(0)[1]()

            @block.scalar
            def _(s):
                w(s, "T", sc)
                guard(s, "p_e0")
                guard(s, "p_e1")
                s.activation(p_e[0][:], psA[:, :384], ACT.Exp, scale=ISQH)
                ins = s.activation(p_e[1][:], psB[:, :384], ACT.Exp, scale=ISQH)
                inc(ins, "C", 1)
                reads(bname(psA), "C", C["C"])
                reads(bname(psB), "C", C["C"])
            ex = C["C"]

            mask0 = masks_bf[:, (qt0 if qt0 < 2 else 2) * 384:(qt0 if qt0 < 2 else 2) * 384 + 384]
            mask1 = masks_bf[:, (qt1 if qt1 < 2 else 2) * 384:(qt1 if qt1 < 2 else 2) * 384 + 384]

            @block.vector
            def _(v):
                w(v, "C", ex)
                guard(v, "p_m0")
                guard(v, "p_m1")
                v.scalar_tensor_tensor(p_m[0][:], p_e[0][:], 0.0, mask0,
                                       op0=ALU.add, op1=ALU.mult, accum_out=rs1[:, 0:1])
                vchain(v, v.scalar_tensor_tensor(p_m[1][:], p_e[1][:], 0.0, mask1,
                                                 op0=ALU.add, op1=ALU.mult,
                                                 accum_out=rs1[:, 2:3]))
                v.reciprocal(rs1[:, 1:2], rs1[:, 0:1])
                vchain(v, v.reciprocal(rs1[:, 3:4], rs1[:, 2:3]))
                v.tensor_scalar_mul(p_m[0][:], p_m[0][:], rs1[:, 1:2])
                ins = v.tensor_scalar_mul(p_m[1][:], p_m[1][:], rs1[:, 3:4])
                inc(ins, "V", 1)
                reads("p_e0", "V", C["V"])
                reads("p_e1", "V", C["V"])
            pn = C["V"]

            items = [(p_m[0][:, c * 128:(c + 1) * 128], at_bf[0][:, c * 128:(c + 1) * 128])
                     for c in range(3)] +                     [(p_m[1][:, c * 128:(c + 1) * 128], at_bf[1][:, c * 128:(c + 1) * 128])
                     for c in range(3)]
            ra = tr_batch(items, [psT0, psT1], pre=[("V", pn)],
                          dst_guards=["at_bf0", "at_bf1"])
            reads("p_m0", "T", ra["lastT"])
            reads("p_m1", "T", ra["lastT"])
            while fillers:
                fillers.pop(0)[1]()

            @block.tensor
            def _(t):
                for s_, v_ in ra["done"].items():
                    w(t, s_, v_)
                guard(t, bname(psC))
                for c in range(3):
                    t.matmul(psC[:, :128], vtm[par][:, (qt0 + c) * 128:(qt0 + c + 1) * 128],
                             at_bf[0][:, c * 128:(c + 1) * 128],
                             start=(c == 0), stop=(c == 2))
                guard(t, bname(psD))
                for c in range(3):
                    ins = t.matmul(psD[:, :128], vtm[par][:, (qt1 + c) * 128:(qt1 + c + 1) * 128],
                                   at_bf[1][:, c * 128:(c + 1) * 128],
                                   start=(c == 0), stop=(c == 2))
                inc(ins, "T", 1)
                reads("at_bf0", "T", C["T"])
                reads("at_bf1", "T", C["T"])
                if qt1 >= NTC - 2:
                    reads(f"vtm{par}", "T", C["T"])
            cm = C["T"]

            tb = qt0 // 4
            tpar = tb % 2

            @block.vector
            def _(v):
                w(v, "T", cm)
                if qt0 % 4 == 0:
                    guard(v, f"ctxo{tpar}")
                v.tensor_copy(ctxo[tpar][:, (qt0 % 4) * 128:(qt0 % 4) * 128 + 128], psC[:, :128])
                ins = v.tensor_copy(ctxo[tpar][:, (qt1 % 4) * 128:(qt1 % 4) * 128 + 128],
                                    psD[:, :128])
                inc(ins, "V", 1)
                reads(bname(psC), "V", C["V"])
                reads(bname(psD), "V", C["V"])
            ce = C["V"]

            if qt1 % 4 == 3:
                @block.sync
                def _(y):
                    y.wait_ge(H["V"], ce)
                    inc(y.dma_start(S_ctx[tb, h], ctxo[tpar][:]), f"st_ctxo{tpar}", 16)
                    reads(f"ctxo{tpar}", f"st_ctxo{tpar}", C[f"st_ctxo{tpar}"])
            pc_pump()

        for gi in range(len(QKV_GROUPS)):
            emit_qkv_group(0, gi)

        for h in range(NH):
            par = h % 2

            if h + 2 < NH:
                @block.gpsimd
                def _(g):
                    for nm, src in head_srcs(h + 2):
                        guard(g, f"wsl_{nm}{par}")
                        inc(g.dma_start(wsl[nm][par][:].rearrange("p (c o) -> p c o", o=128), src),
                            f"ld_wsl{par}", 16)
                    wsl_ld[h + 2] = C[f"ld_wsl{par}"]

            items = [(vf[:, kc * 128:(kc + 1) * 128], vtm[par][:, kc * 128:(kc + 1) * 128])
                     for kc in range(NTHC)]
            rv = tr_batch(items, [psT0, psT1],
                          pre=[qkv_ev[(h, "v", 4)]], dst_guards=[f"vtm{par}"])
            reads("vf", "T", rv["lastT"])

            nxt = list(range(len(QKV_GROUPS))) if h + 1 < NH else []
            for qp in range(NTC // 2):
                fill = []
                for _ in range(2):
                    if nxt:
                        gi = nxt.pop(0)
                        fill.append(("qkv", lambda h1=h + 1, gi=gi: emit_qkv_group(h1, gi)))
                for _ in range(2):
                    if hkq:
                        fill.append(("hk", hkq.pop(0)))
                emit_attn_pair(h, 2 * qp, fill)

        while hkq:
            hkq.pop(0)()
        pc_drain()
        _dump_yfm(block, H, C, inc, y_fm, dbg)
        close_stack(es2)
        close_stack(es_hk)
        close_stack(es_y)
        close_stack(es_pc)
        if upto <= 2:
            _finish(block, H, C)
            close_all()
            return nc
        barrier()

        tb_snap = {"w": None}
        for tb in range(NTB):
            es5o = new_stack()
            es5a = new_stack()
            w_T = sbuf(f"w_T_{tb}", [128, 32 * 512], BF16, es5o)
            p1_fm = sbuf(f"p1_fm_{tb}", [128, DC * 512], BF16, es5o)
            p2_fm = sbuf(f"p2_fm_{tb}", [128, DC * 512], BF16, es5o)
            hk_sl = sbuf(f"hk_sl_{tb}", [128, DC * TBS], BF16, es5a)
            sim_sb = [sbuf(f"sim{i}_{tb}", [128, M], F32, es5a) for i in range(4)]
            es5m = new_stack()
            mbT_sl = [sbuf(f"mbT_sl{i}_{tb}", [128, DC * 512], BF16, es5m) for i in range(2)]

            @block.gpsimd
            def _(g):
                if tb_snap["w"]:
                    for s_, v_ in tb_snap["w"]:
                        w(g, s_, v_)
                guard(g, "hk_sl")
                inc(g.dma_start(hk_sl[:].rearrange("p (c n) -> p c n", n=TBS),
                                S_hk[tb].rearrange("c p n -> p c n")), "ld_hksl", 16)
            hk_ld = C["ld_hksl"]

            sim_ev = {}
            for mb in range(8):
                par = mb % 2

                @block.gpsimd
                def _(g):
                    guard(g, f"mbT_sl{par}")
                    inc(g.dma_start(mbT_sl[par][:].rearrange("p (c n) -> p c n", n=512),
                                    B_mbT[mb]), f"ld_mbT{par}", 16)
                mbT_ld = (f"ld_mbT{par}", C[f"ld_mbT{par}"])

                for tsub in range(4):
                    pp = psAB[tsub % 2]

                    @block.tensor
                    def _(t):
                        if mb == 0 and tsub == 0:
                            w(t, "ld_hksl", hk_ld)
                        w(t, mbT_ld[0], mbT_ld[1])
                        guard(t, bname(pp))
                        for jc in range(DC):
                            ins = t.matmul(pp[:],
                                     hk_sl[:, jc * TBS + tsub * 128: jc * TBS + tsub * 128 + 128],
                                     mbT_sl[par][:, jc * 512:(jc + 1) * 512],
                                     start=(jc == 0), stop=(jc == DC - 1))
                        inc(ins, "T", 1)
                        if tsub == 3:
                            reads(f"mbT_sl{par}", "T", C["T"])
                        if mb == 7 and tsub == 3:
                            reads("hk_sl", "T", C["T"])
                    mm = C["T"]

                    @block.vector
                    def _(v):
                        w(v, "T", mm)
                        guard(v, f"sim{tsub}_{mb}")
                        inc(v.tensor_scalar_mul(sim_sb[tsub][:, mb * 512:(mb + 1) * 512],
                                                pp[:], ISQD), "V", 1)
                        reads(bname(pp), "V", C["V"])
                    sim_ev[(mb, tsub)] = C["V"]

            if debug and tb == 0:
                @block.sync
                def _(y):
                    for tsub in range(4):
                        y.wait_ge(H["V"], sim_ev[(7, tsub)])
                        inc(y.dma_start(dbg["sim0"][tsub], sim_sb[tsub][:]), "st_dbg", 16)
                        for _mb in range(8):
                            reads(f"sim{tsub}_{_mb}", "st_dbg", C["st_dbg"])

            close_stack(es5m)
            mfree = {"T": C["T"], "V": C["V"]}
            es5e = new_stack()
            e_f = sbuf(f"e_f_{tb}", [128, M], F32, es5e)
            w_bf = sbuf(f"w_bf_{tb}", [128, M], BF16, es5e)

            wtr_cp = {}
            for tsub in range(4):
                tc = tb * 4 + tsub

                @block.vector
                def _(v):
                    w(v, "V", sim_ev[(7, tsub)])
                    guard(v, "top8")
                    inc(v.max(out=top8[:], in_=sim_sb[tsub][:]), "V", 1)
                t8_done = C["V"]

                @block.scalar
                def _(s):
                    w(s, "V", sim_ev[(7, tsub)])
                    if tsub == 0:
                        for s_, v_ in mfree.items():
                            w(s, s_, v_)
                    guard(s, "e_f")
                    inc(s.activation(e_f[:], sim_sb[tsub][:], ACT.Exp), "C", 1)
                ee_done = C["C"]

                @block.vector
                def _(v):
                    w(v, "C", ee_done)
                    w(v, "V", t8_done)
                    guard(v, "w_bf")
                    ins = v.scalar_tensor_tensor(w_bf[:], sim_sb[tsub][:], top8[:, 7:8], e_f[:],
                                                 op0=ALU.is_ge, op1=ALU.mult,
                                                 accum_out=rowsums[:, tc:tc + 1])
                    inc(ins, "V", 1)
                    reads("e_f", "V", C["V"])
                    reads("top8", "V", C["V"])
                    for _mb in range(8):
                        reads(f"sim{tsub}_{_mb}", "V", C["V"])
                stt_done = C["V"]

                items = [(w_bf[:, mc * 128:(mc + 1) * 128],
                          w_T[:, mc * 512 + tsub * 128: mc * 512 + tsub * 128 + 128])
                         for mc in range(32)]
                rw = tr_batch(items, [psT0, psT1, psC, psD],
                              pre=[("V", stt_done)],
                              dst_guards=(["w_T"] if tsub == 0 else None))
                reads("w_bf", "T", rw["lastT"])
                wtr_cp[tsub] = dict(rw["done"])

            close_stack(es5e)
            close_stack(es5a)
            free_snap = {"T": C["T"], "V": C["V"], "C": C["C"]}
            es5b = new_stack()
            mb_sl = [sbuf(f"mb_sl{i}_{tb}", [128, M], BF16, es5b) for i in range(2)]
            wv_sl = [sbuf(f"wv_sl{i}_{tb}", [128, D], BF16, es5b) for i in range(2)]

            p1_ev = {}
            for dc in range(DC):
                par = dc % 2

                @block.gpsimd
                def _(g):
                    if dc in (0, 1):
                        for s_, v_ in free_snap.items():
                            w(g, s_, v_)
                    guard(g, f"mb_sl{par}")
                    inc(g.dma_start(mb_sl[par][:].rearrange("p (a c) -> p a c", c=128),
                                    B_mb[dc]), f"ld_mb{par}", 16)
                mb_ld = (f"ld_mb{par}", C[f"ld_mb{par}"])

                pp = psAB[par]

                @block.tensor
                def _(t):
                    w(t, mb_ld[0], mb_ld[1])
                    if dc == 0:
                        for s_, v_ in wtr_cp[3].items():
                            w(t, s_, v_)
                        guard(t, "p1_full")
                    guard(t, bname(pp))
                    for mc in range(32):
                        ins = t.matmul(pp[:], mb_sl[par][:, mc * 128:(mc + 1) * 128],
                                 w_T[:, mc * 512:(mc + 1) * 512],
                                 start=(mc == 0), stop=(mc == 31))
                    inc(ins, "T", 1)
                    reads(f"mb_sl{par}", "T", C["T"])
                    if dc == DC - 1:
                        reads("w_T", "T", C["T"])
                mm = C["T"]

                @block.vector
                def _(v):
                    w(v, "T", mm)
                    inc(v.tensor_copy(p1_fm[:, dc * 512:(dc + 1) * 512], pp[:]), "V", 1)
                    reads(bname(pp), "V", C["V"])
                p1_ev[dc] = C["V"]

            @block.tensor
            def _(t):
                w(t, "V", ones_done)
                for gg, vv in wtr_cp[3].items():
                    w(t, gg, vv)
                guard(t, bname(psF))
                for mc in range(32):
                    ins = t.matmul(psF[0:1, 0:512], ones_bf[:], w_T[:, mc * 512:(mc + 1) * 512],
                                   start=(mc == 0), stop=(mc == 31))
                inc(ins, "T", 1)
            rsr_mm = C["T"]

            @block.vector
            def _(v):
                w(v, "T", rsr_mm)
                guard(v, "rs_row")
                inc(v.tensor_copy(rs_row[0:1, :], psF[0:1, 0:512]), "V", 1)
                reads(bname(psF), "V", C["V"])
            rsr_ev = C["V"]

            p2_ev = {}
            for d2c in range(DC):
                par = d2c % 2

                @block.gpsimd
                def _(g):
                    guard(g, f"wv_sl{par}")
                    inc(g.dma_start(wv_sl[par][:].rearrange("p (c o) -> p c o", o=128),
                                    B_wv[d2c]), f"ld_wv{par}", 16)
                wv_ld = (f"ld_wv{par}", C[f"ld_wv{par}"])

                pp = psCD[par]

                @block.tensor
                def _(t):
                    w(t, wv_ld[0], wv_ld[1])
                    w(t, "V", p1_ev[DC - 1])
                    if d2c == 0:
                        guard(t, "p2_full")
                    guard(t, bname(pp))
                    for dc in range(DC):
                        ins = t.matmul(pp[:], wv_sl[par][:, dc * 128:(dc + 1) * 128],
                                 p1_fm[:, dc * 512:(dc + 1) * 512],
                                 start=(dc == 0), stop=(dc == DC - 1))
                    inc(ins, "T", 1)
                    reads(f"wv_sl{par}", "T", C["T"])
                    if d2c == DC - 1:
                        reads("p1_full", "T", C["T"])
                mm = C["T"]

                @block.vector
                def _(v):
                    w(v, "T", mm)
                    inc(v.tensor_copy(p2_fm[:, d2c * 512:(d2c + 1) * 512], pp[:]), "V", 1)
                    reads(bname(pp), "V", C["V"])
                p2_ev[d2c] = C["V"]

            if debug and tb == 0:
                @block.sync
                def _(y):
                    y.wait_ge(H["V"], p1_ev[DC - 1])
                    inc(y.dma_start(dbg["p1_0"].rearrange("c p n -> p c n"),
                                    p1_fm[:].rearrange("p (c n) -> p c n", n=512)), "st_dbg", 16)
                    reads("p1_full", "st_dbg", C["st_dbg"])
                    y.wait_ge(H["V"], p2_ev[DC - 1])
                    inc(y.dma_start(dbg["p2_0"].rearrange("c p n -> p c n"),
                                    p2_fm[:].rearrange("p (c n) -> p c n", n=512)), "st_dbg", 16)
                    reads("p2_full", "st_dbg", C["st_dbg"])

            woT_bf = sbuf(f"woT_bf_{tb}", [128, DC * D], BF16, es5b)
            ctx_tb = sbuf(f"ctx_tb_{tb}", [128, DC * 512], BF16, es5b)
            ctxp = sbuf(f"ctxp_{tb}", [128, DC * 512], BF16, es5b)
            tmp5 = sbuf(f"tmp5_{tb}", [128, 512], BF16, es5b)
            sbc_bf = sbuf(f"sbc_bf_{tb}", [128, 512], BF16, es5b)
            ao_bf = sbuf(f"ao_bf_{tb}", [128, D], BF16, es5b)

            @block.gpsimd
            def _(g):
                guard(g, "woT_bf")
                for ic in range(DC):
                    inc(g.dma_start(woT_bf[:, ic * D:(ic + 1) * D], B_wo[ic]), "ld_woT", 16)
                guard(g, "ctx_tb")
                inc(g.dma_start(ctx_tb[:].rearrange("p (c n) -> p c n", n=512),
                                S_ctx[tb].rearrange("c p n -> p c n")), "ld_ctxtb", 16)
            wo_ld = C["ld_woT"]
            ctx_ld = C["ld_ctxtb"]

            if tb == 0:
                @block.vector
                def _(v):
                    w(v, "ld_woT", wo_ld)
                    guard(v, "ao_bf")
                    for ic in range(DC):
                        vchain(v, v.tensor_mul(ao_bf[:], woT_bf[:, ic * D:(ic + 1) * D],
                                               gw_bf[:]))
                        vchain(v, v.reduce_sum(gv_sb[:, ic:ic + 1], ao_bf[:], axis=AX))
                    inc(v.tensor_copy(gv_bf[:], gv_sb[:]), "V", 1)
                    reads("ao_bf", "V", C["V"])
                gv_state["done"] = C["V"]
            gv_done = gv_state["done"]

            for tsub in range(4):
                pe = psEF[tsub % 2]

                @block.tensor
                def _(t):
                    if tsub == 0:
                        w(t, "V", gv_done)
                        w(t, "ld_ctxtb", ctx_ld)
                    guard(t, bname(pe))
                    for ic in range(DC):
                        ins = t.matmul(pe[0:1, 0:128], gv_bf[:, ic:ic + 1],
                                       ctx_tb[:, ic * 512 + tsub * 128: ic * 512 + tsub * 128 + 128],
                                       start=(ic == 0), stop=(ic == DC - 1))
                    inc(ins, "T", 1)
                bmm = C["T"]

                @block.vector
                def _(v):
                    w(v, "T", bmm)
                    if tsub == 0:
                        guard(v, "bp_row")
                    inc(v.tensor_copy(bp_row[0:1, tsub * 128:(tsub + 1) * 128], pe[0:1, 0:128]),
                        "V", 1)
                    reads(bname(pe), "V", C["V"])
            bp_done = C["V"]

            @block.vector
            def _(v):
                w(v, "V", bp_done)
                w(v, "V", rsr_ev)
                vchain(v, v.tensor_tensor(bp_row[0:1, :], bp_row[0:1, :],
                                          gb_sb[0:1, 0:1].to_broadcast((1, 512)), op=ALU.add))
                inc(v.reciprocal(rs_row[0:1, :], rs_row[0:1, :]), "V", 1)
            sg0 = C["V"]

            @block.scalar
            def _(s):
                w(s, "V", sg0)
                guard(s, "scl_row")
                inc(s.activation(scl_row[0:1, :], bp_row[0:1, :], ACT.Exp, scale=-1.0), "C", 1)
                reads("bp_row", "C", C["C"])
            sg1 = C["C"]

            @block.vector
            def _(v):
                w(v, "C", sg1)
                vchain(v, v.tensor_scalar_add(scl_row[0:1, :], scl_row[0:1, :], 1.0))
                vchain(v, v.reciprocal(scl_row[0:1, :], scl_row[0:1, :]))
                vchain(v, v.tensor_tensor(scl_row[0:1, :], scl_row[0:1, :], rs_row[0:1, :],
                                          op=ALU.mult))
                inc(v.tensor_copy(sclb_row[0:1, :], scl_row[0:1, :]), "V", 1)
                reads("scl_row", "V", C["V"])
                reads("rs_row", "V", C["V"])
            sg2 = C["V"]

            @block.tensor
            def _(t):
                w(t, "V", sg2)
                w(t, "V", ones_done)
                guard(t, bname(psE))
                ins = t.matmul(psE[:, :512], onesr_bf[:], sclb_row[0:1, :],
                               start=True, stop=True)
                inc(ins, "T", 1)
                reads("sclb_row", "T", C["T"])
            sbc = C["T"]

            @block.vector
            def _(v):
                w(v, "T", sbc)
                w(v, "V", p2_ev[DC - 1])
                guard(v, "ctxp")
                guard(v, "tmp5")
                guard(v, "sbc_bf")
                vchain(v, v.tensor_copy(sbc_bf[:], psE[:, :512]))
                reads(bname(psE), "V", C["V"])
                for ic in range(DC):
                    v.tensor_tensor(tmp5[:], p2_fm[:, ic * 512:(ic + 1) * 512], sbc_bf[:],
                                    op=ALU.mult)
                    ins = v.tensor_tensor(ctxp[:, ic * 512:(ic + 1) * 512], tmp5[:],
                                          ctx_tb[:, ic * 512:(ic + 1) * 512], op=ALU.add)
                inc(ins, "V", 1)
                reads("tmp5", "V", C["V"])
                reads("sbc_bf", "V", C["V"])
                reads("p2_full", "V", C["V"])
                reads("ctx_tb", "V", C["V"])
            ctxp_done = C["V"]

            for tsub in range(4):
                for ob in range(4):
                    pp = psAB[ob % 2]

                    @block.tensor
                    def _(t):
                        if tsub == 0 and ob == 0:
                            w(t, "ld_woT", wo_ld)
                            w(t, "V", ctxp_done)
                        guard(t, bname(pp))
                        for ic in range(DC):
                            ins = t.matmul(pp[:],
                                           ctxp[:, ic * 512 + tsub * 128: ic * 512 + tsub * 128 + 128],
                                           woT_bf[:, ic * D + ob * 512: ic * D + (ob + 1) * 512],
                                           start=(ic == 0), stop=(ic == DC - 1))
                        inc(ins, "T", 1)
                        if tsub == 3 and ob == 3:
                            reads("ctxp", "T", C["T"])
                            reads("woT_bf", "T", C["T"])
                    mm = C["T"]

                    @block.vector
                    def _(v):
                        w(v, "T", mm)
                        if ob == 0:
                            guard(v, "ao_bf")
                        inc(v.tensor_copy(ao_bf[:, ob * 512:(ob + 1) * 512], pp[:]), "V", 1)
                        reads(bname(pp), "V", C["V"])
                    comb3 = C["V"]

                @block.sync
                def _(y):
                    y.wait_ge(H["V"], comb3)
                    inc(y.dma_start(S_ao[tb, tsub], ao_bf[:]), "st_ao", 16)
                    reads("ao_bf", "st_ao", C["st_ao"])

            close_stack(es5b)
            close_stack(es5o)
            tb_snap["w"] = [("T", C["T"]), ("V", C["V"]), ("st_ao", C["st_ao"])]

        barrier()
        if upto <= 3:
            _finish(block, H, C)
            close_all()
            return nc

        es67 = new_stack()
        y2_fm = sbuf("y2_fm", [128, DC * T], BF16)
        W1_bf = sbuf("W1_bf", [128, DC * D], BF16, es67)
        h_bf = sbuf("h_bf", [128, DC * 512], BF16, es67)
        lg8 = sbuf("lg8", [128, EXP], F32, es67)
        ee8 = sbuf("ee8", [128, EXP], F32, es67)
        nmx = sbuf("nmx", [128, 2], F32, es67)
        xt6 = [sbuf(f"xt6{i}", [128, D], F32, es67) for i in range(2)]
        ao6 = [sbuf(f"ao6{i}", [128, D], BF16, es67) for i in range(2)]
        x1t = sbuf("x1t", [128, D], F32, es67)
        x1b = sbuf("x1b", [128, D], BF16, es67)
        y2b = [sbuf(f"y2b{i}", [128, D], BF16, es67) for i in range(2)]

        @block.gpsimd
        def _(g):
            for dc in range(DC):
                inc(g.dma_start(W1_bf[:, dc * D:(dc + 1) * D],
                                W1_nat[dc * 128:(dc + 1) * 128, :]), "ld_w1", 16)
        w1_ld = C["ld_w1"]

        def emit_ln2(tc):
            par = tc % 2
            tb, tsub = tc // 4, tc % 4

            @block.gpsimd
            def _(g):
                guard(g, f"xt6{par}")
                guard(g, f"ao6{par}")
                inc(g.dma_start(xt6[par][:], x_in[HALO + tc * 128: HALO + (tc + 1) * 128, :]),
                    f"ld_p6{par}", 16)
                inc(g.dma_start(ao6[par][:], S_ao[tb, tsub]), f"ld_p6{par}", 16)
            x6_ld = C[f"ld_p6{par}"]

            @block.vector
            def _(v):
                w(v, f"ld_p6{par}", x6_ld)
                guard(v, "x1t")
                guard(v, "x1b")
                v.tensor_tensor(x1t[:], xt6[par][:], ao6[par][:], op=ALU.add)
                v.tensor_copy(x1b[:], x1t[:])
                vchain(v, v.reduce_sum(stat[:, 0:1], x1t[:], axis=AX))
                inc(v.tensor_scalar_mul(stat[:, 1:2], stat[:, 0:1], 1.0 / D), "V", 1)
                reads(f"xt6{par}", "V", C["V"])
                reads(f"ao6{par}", "V", C["V"])
            m6 = C["V"]

            @block.scalar
            def _(s):
                w(s, "V", m6)
                guard(s, "st23")
                inc(s.activation(xt6[par][:], x1t[:], ACT.Square, accum_out=stat[:, 2:3]), "C", 1)
                reads(f"xt6{par}", "C", C["C"])
            sq6 = C["C"]

            @block.vector
            def _(v):
                w(v, "C", sq6)
                w(v, "V", m6)
                vchain(v, v.tensor_scalar(stat[:, 4:5], stat[:, 1:2], stat[:, 1:2], EPS,
                                          op0=ALU.mult, op1=ALU.subtract))
                inc(v.tensor_scalar(stat[:, 5:6], stat[:, 2:3], 1.0 / D, stat[:, 4:5],
                                    op0=ALU.mult, op1=ALU.subtract), "V", 1)
                reads("st23", "V", C["V"])
            v6 = C["V"]

            @block.scalar
            def _(s):
                w(s, "V", v6)
                guard(s, "st67")
                inc(s.activation(stat[:, 6:7], stat[:, 5:6], ACT.Sqrt), "C", 1)
            sd6 = C["C"]

            @block.vector
            def _(v):
                w(v, "C", sd6)
                guard(v, f"y2b{par}")
                vchain(v, v.reciprocal(stat[:, 7:8], stat[:, 6:7]))
                ins = v.tensor_scalar(y2b[par][:], x1t[:], stat[:, 1:2], stat[:, 7:8],
                                      op0=ALU.subtract, op1=ALU.mult)
                inc(ins, "V", 1)
                reads("x1t", "V", C["V"])
                reads("st67", "V", C["V"])
            n6 = C["V"]

            @block.sync
            def _(y):
                y.wait_ge(H["V"], m6)
                inc(y.dma_start(S_x1[tc], x1b[:]), "st_x1", 16)
                reads("x1b", "st_x1", C["st_x1"])

            items = [(y2b[par][:, dc * 128:(dc + 1) * 128],
                      y2_fm[:, dc * T + tc * 128: dc * T + tc * 128 + 128])
                     for dc in range(DC)]
            r6 = tr_batch(items, [psT0, psT1], pre=[("V", n6)])
            reads(f"y2b{par}", "T", r6["lastT"])

        def emit_h_block(tb, y2v):
            h_ev = {}
            for ec in range(DC):
                pp = psAB[ec % 2]

                @block.tensor
                def _(t):
                    if ec == 0:
                        w(t, "V", y2v)
                        if tb == 0:
                            w(t, "ld_w1", w1_ld)
                    guard(t, bname(pp))
                    guard(t, f"h{ec}")
                    for dc in range(DC):
                        ins = t.matmul(pp[:],
                                       W1_bf[:, dc * D + ec * 128: dc * D + ec * 128 + 128],
                                       y2_fm[:, dc * T + tb * TBS: dc * T + (tb + 1) * TBS],
                                       start=(dc == 0), stop=(dc == DC - 1))
                    inc(ins, "T", 1)
                mm = C["T"]

                @block.vector
                def _(v):
                    w(v, "T", mm)
                    ins = v.tensor_scalar(h_bf[:, ec * 512:(ec + 1) * 512], pp[:],
                                          b1_sb[:, ec:ec + 1], 0.0,
                                          op0=ALU.add, op1=ALU.max)
                    inc(ins, "V", 1)
                    reads(bname(pp), "V", C["V"])
                h_ev[ec] = C["V"]
            return h_ev

        def emit_logits(tb, h_ev):
            for tsub in range(4):
                tc = tb * 4 + tsub
                pp = psEF[tsub % 2]

                @block.tensor
                def _(t):
                    w(t, "V", h_ev[DC - 1])
                    guard(t, bname(pp))
                    for ec in range(DC):
                        ins = t.matmul(pp[:, :EXP],
                                       h_bf[:, ec * 512 + tsub * 128: ec * 512 + tsub * 128 + 128],
                                       W2_bf[:, ec * EXP:(ec + 1) * EXP],
                                       start=(ec == 0), stop=(ec == DC - 1))
                    inc(ins, "T", 1)
                    if tsub == 3:
                        for ec in range(DC):
                            reads(f"h{ec}", "T", C["T"])
                lgm = C["T"]

                @block.vector
                def _(v):
                    w(v, "T", lgm)
                    guard(v, "lg8")
                    guard(v, "nmx0")
                    vchain(v, v.tensor_tensor(lg8[:], pp[:, :EXP], b2_sb[:], op=ALU.add))
                    inc(v.reduce_max(nmx[:, 0:1], lg8[:], axis=AX, negate=True), "V", 1)
                    reads(bname(pp), "V", C["V"])
                lgv = C["V"]

                @block.scalar
                def _(s):
                    w(s, "V", lgv)
                    guard(s, "ee8")
                    inc(s.activation(ee8[:], lg8[:], ACT.Exp, bias=nmx[:, 0:1]), "C", 1)
                    reads("lg8", "C", C["C"])
                    reads("nmx0", "C", C["C"])
                eev = C["C"]

                @block.vector
                def _(v):
                    w(v, "C", eev)
                    vchain(v, v.reduce_sum(nmx[:, 1:2], ee8[:], axis=AX))
                    vchain(v, v.reciprocal(nmx[:, 1:2], nmx[:, 1:2]))
                    ins = v.tensor_scalar_mul(alpha_sb[:, tc * EXP:(tc + 1) * EXP],
                                              ee8[:], nmx[:, 1:2])
                    inc(ins, "V", 1)
                    reads("ee8", "V", C["V"])

        prev = None
        for tb in range(NTB):
            for tsub in range(4):
                emit_ln2(tb * 4 + tsub)
            y2v = C["V"]
            if prev is not None:
                emit_logits(prev[0], prev[1])
            h_ev = emit_h_block(tb, y2v)
            prev = (tb, h_ev)
        emit_logits(prev[0], prev[1])

        close_stack(es67)
        if debug:
            @block.sync
            def _(y):
                y.wait_ge(H["V"], C["V"])
                inc(y.dma_start(dbg["y2fm"].rearrange("c p n -> p c n"),
                                y2_fm[:].rearrange("p (c n) -> p c n", n=T)), "st_dbg", 16)
                inc(y.dma_start(dbg["alpha"], alpha_sb[:]), "st_dbg", 16)
                inc(y.dma_start(dbg["rs"], rowsums[:]), "st_dbg", 16)
        if upto <= 5:
            _finish(block, H, C)
            close_all()
            return nc
        barrier()

        es8 = new_stack()
        VT_bf = sbuf("VT_bf", [128, DC * 512], BF16, es8)
        UT_bf = sbuf("UT_bf", [128, 4 * D], BF16, es8)
        avx_bf = sbuf("avx_bf", [128, 512], BF16, es8)
        avx_fm = sbuf("avx_fm", [128, 512], BF16, es8)
        x1in = [sbuf(f"x1in{i}", [128, D], BF16, es8) for i in range(2)]
        outt = [sbuf(f"outt{i}", [128, D], F32, es8) for i in range(2)]

        @block.gpsimd
        def _(g):
            for dc in range(DC):
                inc(g.dma_start(VT_bf[:, dc * 512:(dc + 1) * 512],
                                VT_in[dc * 128:(dc + 1) * 128, :]), "ld_vut", 16)
            for krc in range(4):
                inc(g.dma_start(UT_bf[:, krc * D:(krc + 1) * D],
                                UT_in[krc * 128:(krc + 1) * 128, :]), "ld_vut", 16)
        vu_ld = C["ld_vut"]

        for tc in range(NTC):
            par = tc % 2
            pv = psAB[tc % 2]

            @block.gpsimd
            def _(g):
                guard(g, f"x1in{par}")
                inc(g.dma_start(x1in[par][:], S_x1[tc]), f"ld_x1in{par}", 16)
            x1_ld = C[f"ld_x1in{par}"]

            @block.tensor
            def _(t):
                if tc == 0:
                    w(t, "ld_vut", vu_ld)
                guard(t, bname(pv))
                for dc in range(DC):
                    ins = t.matmul(pv[:], y2_fm[:, dc * T + tc * 128: dc * T + tc * 128 + 128],
                             VT_bf[:, dc * 512:(dc + 1) * 512],
                             start=(dc == 0), stop=(dc == DC - 1))
                inc(ins, "T", 1)
            vxm = C["T"]

            @block.vector
            def _(v):
                w(v, "T", vxm)
                guard(v, "avx_bf")
                for k in range(EXP):
                    ins = v.tensor_scalar_mul(avx_bf[:, k * R:(k + 1) * R],
                                              pv[:, k * R:(k + 1) * R],
                                              alpha_sb[:, tc * EXP + k: tc * EXP + k + 1])
                inc(ins, "V", 1)
                reads(bname(pv), "V", C["V"])
            avxd = C["V"]

            items = [(avx_bf[:, c * 128:(c + 1) * 128],
                      avx_fm[:, c * 128:(c + 1) * 128]) for c in range(4)]
            r8 = tr_batch(items, [psT0, psT1],
                          pre=[("V", avxd)], dst_guards=["avx_fm"])
            reads("avx_bf", "T", r8["lastT"])
            avcp8 = dict(r8["done"])

            out_done = {}
            for db in range(4):
                pp = psCD[db % 2]

                @block.tensor
                def _(t):
                    for s_, v_ in avcp8.items():
                        w(t, s_, v_)
                    guard(t, bname(pp))
                    for krc in range(4):
                        ins = t.matmul(pp[:], avx_fm[:, krc * 128:(krc + 1) * 128],
                                 UT_bf[:, krc * D + db * 512: krc * D + (db + 1) * 512],
                                 start=(krc == 0), stop=(krc == 3))
                    inc(ins, "T", 1)
                    if db == 3:
                        reads("avx_fm", "T", C["T"])
                ofm = C["T"]

                @block.vector
                def _(v):
                    w(v, "T", ofm)
                    w(v, f"ld_x1in{par}", x1_ld)
                    guard(v, f"outt{par}_{db}")
                    ins = v.tensor_tensor(outt[par][:, db * 512:(db + 1) * 512], pp[:],
                                          x1in[par][:, db * 512:(db + 1) * 512], op=ALU.add)
                    inc(ins, "V", 1)
                    reads(bname(pp), "V", C["V"])
                    if db == 3:
                        reads(f"x1in{par}", "V", C["V"])
                out_done[db] = C["V"]

            @block.sync
            def _(y):
                y.wait_ge(H["V"], out_done[3])
                inc(y.dma_start(x_out[tc * 128:(tc + 1) * 128, :], outt[par][:]), "st_out", 16)
                for db in range(4):
                    reads(f"outt{par}_{db}", "st_out", C["st_out"])

        close_stack(es8)
        _finish(block, H, C)
        close_all()
    return nc


def _dump_yfm(block, H, C, inc, y_fm, dbg):
    if "yfm" not in dbg:
        return

    @block.sync
    def _(y):
        y.wait_ge(H["V"], C["V"])
        y.wait_ge(H["C"], C["C"])
        inc(y.dma_start(dbg["yfm"].rearrange("c p n -> p c n"),
                        y_fm[:].rearrange("p (c n) -> p c n", n=TH)), "st_dbg", 16)


def _finish(block, H, C):
    @block.sync
    def _(y):
        for s, v in C.items():
            if v > 0:
                y.wait_ge(H[s], v)


def _prep_inputs(inputs):
    f = np.float32
    hs = np.asarray(inputs["L0_attn_hidden_states"], dtype=f)
    wq = np.asarray(inputs["L0_attn_wq"], dtype=f)
    wk = np.asarray(inputs["L0_attn_wk"], dtype=f)
    wv = np.asarray(inputs["L0_attn_wv"], dtype=f)
    wo = np.asarray(inputs["L0_attn_wo"], dtype=f)
    gate_w = np.asarray(inputs["L0_attn_gate_w"], dtype=f)
    gate_b = np.asarray(inputs["L0_attn_gate_b"], dtype=f)
    mem_bank = np.asarray(inputs["L0_attn_mem_bank"], dtype=f)
    U = np.asarray(inputs["L0_ffn_U"], dtype=f)
    V = np.asarray(inputs["L0_ffn_V"], dtype=f)
    W1 = np.asarray(inputs["L0_ffn_W1"], dtype=f)
    b1 = np.asarray(inputs["L0_ffn_b1"], dtype=f)
    W2 = np.asarray(inputs["L0_ffn_W2"], dtype=f)
    b2 = np.asarray(inputs["L0_ffn_b2"], dtype=f)

    def blk_T(wm):
        return np.ascontiguousarray(wm.T.reshape(D, NH, HD).transpose(1, 0, 2))

    shared = {
        "wqT_h": blk_T(wq), "wkT_h": blk_T(wk), "wvT_h": blk_T(wv),
        "wk_nat": np.ascontiguousarray(wk),
        "woT": np.ascontiguousarray(wo.T),
        "mbT_pb": np.ascontiguousarray(
            mem_bank.T.reshape(DC, 128, 8, 512).transpose(2, 1, 0, 3)),
        "mb_pb": np.ascontiguousarray(
            mem_bank.reshape(32, 128, DC, 128).transpose(2, 1, 0, 3)),
        "wvT_pb": np.ascontiguousarray(
            wv.T.reshape(DC, 128, DC, 128).transpose(2, 1, 0, 3)),
        "W1_nat": np.ascontiguousarray(W1),
        "b1c": np.ascontiguousarray(b1.reshape(DC, 128).T),
        "W2r": np.ascontiguousarray(W2.reshape(DC, 128, EXP)),
        "b2b": np.ascontiguousarray(np.broadcast_to(b2[None, :], (128, EXP))),
        "gw_rep": np.ascontiguousarray(np.broadcast_to(gate_w.reshape(1, D), (128, D))),
        "gb_rep": np.full((128, 1), float(np.asarray(gate_b).reshape(-1)[0]), dtype=f),
        "VT_in": np.ascontiguousarray(V.transpose(2, 0, 1).reshape(D, EXP * R)),
        "UT_in": np.ascontiguousarray(U.transpose(0, 2, 1).reshape(EXP * R, D)),
        "iden_in": np.eye(128, dtype=f),
    }

    r = np.arange(128)[:, None]
    c = np.arange(384)[None, :]
    m_main = ((c > r) & (c <= r + WIN)).astype(f)
    mask_even = np.stack([m_main * (c >= 256), m_main * (c >= 128), m_main])
    mask_odd = np.stack([m_main, m_main, m_main])

    in_maps = []
    Tq = hs.shape[1]
    for core in range(8):
        b, half = core // 2, core % 2
        if half == 0:
            xc = np.concatenate([np.zeros((HALO, D), dtype=f), hs[b, :T]], axis=0)
            mk = mask_even
        else:
            xc = hs[b, T - HALO: Tq]
            mk = mask_odd
        m = dict(shared)
        m["x_in"] = np.ascontiguousarray(xc)
        m["mask3"] = np.ascontiguousarray(mk)
        in_maps.append(m)
    return in_maps, hs.shape


def kernel(**inputs):
    in_maps, full_shape = _prep_inputs(inputs)
    nc = build_program(debug=False)
    res = run_bass_kernel_spmd(nc, in_maps, core_ids=list(range(8)))
    B, Tq, _ = full_shape
    out = np.empty((B, Tq, D), dtype=np.float32)
    for core in range(8):
        b, half = core // 2, core % 2
        out[b, half * T:(half + 1) * T] = res.results[core]["x_out"]
    return out


# version 17
# speedup vs baseline: 1.0077x; 1.0024x over previous
import numpy as np
import concourse.bass as bass
import concourse.mybir as mybir
from concourse.bass_utils import run_bass_kernel_spmd
from contextlib import ExitStack

F32 = mybir.dt.float32
BF16 = mybir.dt.bfloat16
AX = mybir.AxisListType.X
ALU = mybir.AluOpType
ACT = mybir.ActivationFunctionType

D = 2048
T = 2048
HALO = 256
TH = T + HALO
NH = 16
HD = 128
WIN = 256
M = 4096
EXP = 8
R = 64
DC = D // 128
TBS = 512
NTB = T // TBS
NTC = T // 128
NTHC = TH // 128
ISQD = float(1.0 / np.sqrt(D))
ISQH = float(1.0 / np.sqrt(HD))
EPS = 1e-5
P0_TILES = NTHC
HK_JCS = DC


def build_program(debug=False, upto=99):
    nc = bass.Bass("TRN2", target_bir_lowering=False)
    dt_dbg = "ExternalOutput" if debug else "Internal"

    def din(name, shape):
        return nc.dram_tensor(name, shape, F32, kind="ExternalInput").ap()

    x_in = din("x_in", [TH, D])
    wqT_h = din("wqT_h", [NH, D, HD])
    wkT_h = din("wkT_h", [NH, D, HD])
    wvT_h = din("wvT_h", [NH, D, HD])
    wk_nat = din("wk_nat", [D, D])
    woT = din("woT", [D, D])
    mbT_pb = din("mbT_pb", [8, 128, DC, 512])
    mb_pb = din("mb_pb", [DC, 128, 32, 128])
    wvT_pb = din("wvT_pb", [DC, 128, DC, 128])
    W1_nat = din("W1_nat", [D, D])
    b1c = din("b1c", [128, DC])
    W2r = din("W2r", [DC, 128, EXP])
    b2b = din("b2b", [128, EXP])
    gw_rep = din("gw_rep", [128, D])
    gb_rep = din("gb_rep", [128, 1])
    VT_in = din("VT_in", [D, EXP * R])
    UT_in = din("UT_in", [EXP * R, D])
    iden_in = din("iden_in", [128, 128])
    mask3 = din("mask3", [3, 128, 384])

    x_out = nc.dram_tensor("x_out", [T, D], F32, kind="ExternalOutput").ap()

    S_hk = nc.dram_tensor("S_hk", [NTB, DC, 128, TBS], BF16, kind=dt_dbg).ap()
    B_mbT = nc.dram_tensor("B_mbT", [8, 128, DC, 512], BF16, kind="Internal").ap()
    B_mb = nc.dram_tensor("B_mb", [DC, 128, 32, 128], BF16, kind="Internal").ap()
    B_wv = nc.dram_tensor("B_wv", [DC, 128, DC, 128], BF16, kind="Internal").ap()
    B_wo = nc.dram_tensor("B_wo", [DC, 128, D], BF16, kind="Internal").ap()
    S_ctx = nc.dram_tensor("S_ctx", [NTB, NH, 128, 512], BF16, kind=dt_dbg).ap()
    S_ao = nc.dram_tensor("S_ao", [NTB, 4, 128, D], BF16, kind=dt_dbg).ap()
    S_x1 = nc.dram_tensor("S_x1", [NTC, 128, D], BF16, kind=dt_dbg).ap()
    dbg = {}
    if debug:
        dbg["yfm"] = nc.dram_tensor("D_yfm", [DC, 128, TH], BF16, kind="ExternalOutput").ap()
        dbg["sim0"] = nc.dram_tensor("D_sim0", [4, 128, M], F32, kind="ExternalOutput").ap()
        dbg["p1_0"] = nc.dram_tensor("D_p1_0", [DC, 128, TBS], BF16, kind="ExternalOutput").ap()
        dbg["p2_0"] = nc.dram_tensor("D_p2_0", [DC, 128, TBS], BF16, kind="ExternalOutput").ap()
        dbg["rs"] = nc.dram_tensor("D_rs", [128, NTC], F32, kind="ExternalOutput").ap()
        dbg["alpha"] = nc.dram_tensor("D_alpha", [128, NTC * EXP], F32, kind="ExternalOutput").ap()
        dbg["y2fm"] = nc.dram_tensor("D_y2fm", [DC, 128, T], BF16, kind="ExternalOutput").ap()

    es = ExitStack()
    open_stacks = []

    def new_stack():
        st = ExitStack()
        open_stacks.append(st)
        return st

    def close_stack(st):
        assert open_stacks and open_stacks[-1] is st, "non-LIFO stack close"
        open_stacks.pop()
        st.close()

    def close_all():
        while open_stacks:
            open_stacks.pop().close()

    with es:
        def sbuf(name, shape, dt, stack=None):
            return (stack or es).enter_context(nc.sbuf_tensor(name, shape, dt))

        iden_bf = sbuf("iden_bf", [128, 128], BF16)
        masks_bf = sbuf("masks_bf", [128, 3 * 384], BF16)
        b1_sb = sbuf("b1_sb", [128, DC], F32)
        W2_bf = sbuf("W2_bf", [128, DC * EXP], BF16)
        b2_sb = sbuf("b2_sb", [128, EXP], F32)
        gw_bf = sbuf("gw_bf", [128, D], BF16)
        gb_sb = sbuf("gb_sb", [128, 1], F32)
        rowsums = sbuf("rowsums", [128, NTC], F32)
        alpha_sb = sbuf("alpha_sb", [128, NTC * EXP], F32)
        stat = sbuf("stat", [128, 16], F32)
        top8 = sbuf("top8", [128, 8], F32)
        ones_bf = sbuf("ones_bf", [128, 1], BF16)
        onesr_bf = sbuf("onesr_bf", [1, 128], BF16)
        gv_sb = sbuf("gv_sb", [128, DC], F32)
        gv_bf = sbuf("gv_bf", [128, DC], BF16)
        bp_row = sbuf("bp_row", [1, 512], F32)
        rs_row = sbuf("rs_row", [1, 512], F32)
        scl_row = sbuf("scl_row", [1, 512], F32)
        sclb_row = sbuf("sclb_row", [1, 512], BF16)

        psA = es.enter_context(nc.psum_tensor("psA", [128, 512], F32))
        psB = es.enter_context(nc.psum_tensor("psB", [128, 512], F32))
        psC = es.enter_context(nc.psum_tensor("psC", [128, 512], F32))
        psD = es.enter_context(nc.psum_tensor("psD", [128, 512], F32))
        psE = es.enter_context(nc.psum_tensor("psE", [128, 512], F32))
        psF = es.enter_context(nc.psum_tensor("psF", [128, 512], F32))
        psT0 = es.enter_context(nc.psum_tensor("psT0", [128, 512], F32))
        psT1 = es.enter_context(nc.psum_tensor("psT1", [128, 512], F32))
        psAB = [psA, psB]
        psCD = [psC, psD]
        psEF = [psE, psF]
        psT = [psT0, psT1]
        PSNAME = {id(psA): "psA", id(psB): "psB", id(psC): "psC", id(psD): "psD",
                  id(psE): "psE", id(psF): "psF", id(psT0): "psT0", id(psT1): "psT1"}

        gv_state = {"done": 0}
        H = {s: es.enter_context(nc.semaphore(s)) for s in ("T", "V", "C")}
        C = {s: 0 for s in H}
        ENG_SEM = {"tensor": "T", "vector": "V", "scalar": "C", "gpsimd": None, "sync": None}
        block = es.enter_context(nc.Block())

        def sem_of(s):
            if s not in H:
                H[s] = es.enter_context(nc.semaphore(f"d_{s}"))
                C[s] = 0
            return H[s]

        def inc(instr, s, n):
            instr.then_inc(sem_of(s), n)
            C[s] += n
            return C[s]

        def w(eng, s, v):
            if v > 0:
                eng.wait_ge(sem_of(s), v)

        def vchain(v, ins):
            c = inc(ins, "V", 1)
            v.wait_ge(H["V"], c)

        def tr_batch(items, banks, evac="vector", pre=None, dst_guards=None):
            gd = {}
            for k in (dst_guards or []):
                for s_, v_ in (readers.pop(k, None) or {}).items():
                    gd[s_] = max(gd.get(s_, 0), v_)
            eng_first = {"vector": True, "scalar": True}
            done = {}
            lastT = 0
            ncyc = (len(items) + 3) // 4
            for ci in range(ncyc):
                grp = items[ci * 4:(ci + 1) * 4]
                b = banks[ci % len(banks)]

                def tbody(t, grp=grp, b=b, first=(ci == 0)):
                    if first and pre:
                        for s_, v_ in pre:
                            w(t, s_, v_)
                    guard(t, bname(b))
                    for c, (srcap, dstap) in enumerate(grp):
                        ins = t.matmul(b[:, c * 128:(c + 1) * 128], srcap, iden_bf[:],
                                       start=True, stop=True)
                    inc(ins, "T", 1)
                block.tensor(tbody)
                lastT = C["T"]

                engname = ("vector" if (ci % 2 == 0) else "scalar") if evac == "alt" else evac
                sem = "V" if engname == "vector" else "C"

                def ebody(e, grp=grp, b=b, mm=lastT, engname=engname, sem=sem):
                    w(e, "T", mm)
                    if eng_first[engname]:
                        eng_first[engname] = False
                        for s_, v_ in gd.items():
                            w(e, s_, v_)
                    ins = None
                    for c, (srcap, dstap) in enumerate(grp):
                        if engname == "vector":
                            ins = e.tensor_copy(dstap, b[:, c * 128:(c + 1) * 128])
                        else:
                            ins = e.copy(dstap, b[:, c * 128:(c + 1) * 128])
                    inc(ins, sem, 1)
                    reads(bname(b), sem, C[sem])
                getattr(block, engname)(ebody)
                done[sem] = C[sem]
            return {"lastT": lastT, "done": done}

        readers = {}

        def reads(key, s, v):
            d = readers.setdefault(key, {})
            d[s] = max(d.get(s, 0), v)

        def guard(eng, key):
            d = readers.pop(key, None)
            if d:
                for s, v in d.items():
                    w(eng, s, v)

        def bname(ps):
            return PSNAME[id(ps)]

        def barrier():
            snap = dict(C)
            for engname in ("tensor", "vector", "scalar", "gpsimd", "sync"):
                own = ENG_SEM[engname]

                def body(e, snap=snap, own=own):
                    for s, v in snap.items():
                        if s != own and v > 0:
                            e.wait_ge(H[s], v)
                getattr(block, engname)(body)

        es_pc = new_stack()
        pcb = [sbuf(f"pcb{i}", [128, 4096], BF16, es_pc) for i in range(2)]
        pc_units = ([("mbT", (j, hh)) for j in range(8) for hh in range(2)] +
                    [("mb", j) for j in range(DC)] +
                    [("wv", j) for j in range(DC)] +
                    [("wo", j) for j in range(DC)])
        pc_state = {"i": 0, "s": 0}
        pc_ld = {}

        def pc_pump():
            i = pc_state["s"]
            if i < pc_state["i"]:
                kind, j = pc_units[i]
                par = i % 2
                nsz = {"mbT": 4096, "mb": 4096, "wv": 2048, "wo": 2048}[kind]
                if kind == "mbT":
                    jj, hh = j
                    dst = B_mbT[jj].rearrange("p c n -> p (c n)")[:, hh * 4096:(hh + 1) * 4096]
                else:
                    dst = {"mb": lambda: B_mb[j].rearrange("p c n -> p (c n)"),
                           "wv": lambda: B_wv[j].rearrange("p c n -> p (c n)"),
                           "wo": lambda: B_wo[j]}[kind]()

                def sbody(y, i=i, par=par, nsz=nsz, dst=dst):
                    w(y, f"ld_pcb{par}", pc_ld[i])
                    inc(y.dma_start(dst, pcb[par][:, :nsz]), f"st_pcb{par}", 16)
                    reads(f"pcb{par}", f"st_pcb{par}", C[f"st_pcb{par}"])
                block.sync(sbody)
                pc_state["s"] = i + 1
            i = pc_state["i"]
            if i < len(pc_units):
                kind, j = pc_units[i]
                par = i % 2
                if kind == "mbT":
                    jj, hh = j
                    src_ap = mbT_pb[jj].rearrange("p c n -> p (c n)")[:, hh * 4096:(hh + 1) * 4096]
                else:
                    src_ap = {"mb": lambda: mb_pb[j].rearrange("p c n -> p (c n)"),
                              "wv": lambda: wvT_pb[j].rearrange("p c n -> p (c n)"),
                              "wo": lambda: woT[j * 128:(j + 1) * 128, :]}[kind]()
                nsz = {"mbT": 4096, "mb": 4096, "wv": 2048, "wo": 2048}[kind]

                def lbody(g, i=i, par=par, src_ap=src_ap, nsz=nsz):
                    guard(g, f"pcb{par}")
                    pc_ld[i] = inc(g.dma_start(pcb[par][:, :nsz], src_ap), f"ld_pcb{par}", 16)
                block.gpsimd(lbody)
                pc_state["i"] = i + 1

        def pc_drain():
            while pc_state["s"] < len(pc_units):
                pc_pump()

        @block.gpsimd
        def _(g):
            inc(g.dma_start(iden_bf[:], iden_in), "consts", 16)
            inc(g.dma_start(masks_bf[:].rearrange("p (i n) -> p i n", n=384), mask3.rearrange("i p n -> p i n")), "consts", 16)
            inc(g.dma_start(b1_sb[:], b1c), "consts", 16)
            inc(g.dma_start(W2_bf[:].rearrange("p (c k) -> p c k", k=EXP), W2r.rearrange("c p k -> p c k")), "consts", 16)
            inc(g.dma_start(b2_sb[:], b2b), "consts", 16)
            inc(g.dma_start(gw_bf[:], gw_rep), "consts", 16)
            inc(g.dma_start(gb_sb[:], gb_rep), "consts", 16)
        consts_ld = C["consts"]

        @block.vector
        def _(v):
            v.memset(ones_bf[:], 1.0)
            inc(v.memset(onesr_bf[:], 1.0), "V", 1)
        ones_done = C["V"]

        es_y = new_stack()
        es0 = new_stack()
        y_fm = sbuf("y_fm", [128, DC * TH], BF16, es_y)
        xt = [sbuf(f"xt{i}", [128, D], F32, es0) for i in range(2)]
        junk = sbuf("junk", [128, D], F32, es0)
        ybf = [sbuf(f"ybf{i}", [128, D], BF16, es0) for i in range(2)]

        for thc in range(P0_TILES):
            par = thc % 2

            @block.gpsimd
            def _(g):
                guard(g, f"xt{par}")
                inc(g.dma_start(xt[par][:], x_in[thc * 128:(thc + 1) * 128, :]), f"ld_xt{par}", 16)
            x_ld = C[f"ld_xt{par}"]
            pc_pump()

            @block.vector
            def _(v):
                w(v, f"ld_xt{par}", x_ld)
                vchain(v, v.reduce_sum(stat[:, 0:1], xt[par][:], axis=AX))
                inc(v.tensor_scalar_mul(stat[:, 1:2], stat[:, 0:1], 1.0 / D), "V", 1)
            mean_done = C["V"]

            @block.scalar
            def _(s):
                w(s, f"ld_xt{par}", x_ld)
                guard(s, "st23")
                inc(s.activation(junk[:], xt[par][:], ACT.Square, accum_out=stat[:, 2:3]), "C", 1)
            sq_done = C["C"]

            @block.vector
            def _(v):
                w(v, "C", sq_done)
                w(v, "V", mean_done)
                vchain(v, v.tensor_scalar(stat[:, 4:5], stat[:, 1:2], stat[:, 1:2], EPS,
                                          op0=ALU.mult, op1=ALU.subtract))
                inc(v.tensor_scalar(stat[:, 5:6], stat[:, 2:3], 1.0 / D, stat[:, 4:5],
                                    op0=ALU.mult, op1=ALU.subtract), "V", 1)
                reads("st23", "V", C["V"])
            var_done = C["V"]

            @block.scalar
            def _(s):
                w(s, "V", var_done)
                guard(s, "st67")
                inc(s.activation(stat[:, 6:7], stat[:, 5:6], ACT.Sqrt), "C", 1)
            sd_done = C["C"]

            @block.vector
            def _(v):
                w(v, "C", sd_done)
                guard(v, f"ybf{par}")
                vchain(v, v.reciprocal(stat[:, 7:8], stat[:, 6:7]))
                ins = v.tensor_scalar(ybf[par][:], xt[par][:], stat[:, 1:2], stat[:, 7:8],
                                      op0=ALU.subtract, op1=ALU.mult)
                inc(ins, "V", 1)
                reads(f"xt{par}", "V", C["V"])
                reads("st67", "V", C["V"])
            norm_done = C["V"]

            pre0 = [("V", norm_done)]
            if thc == 0:
                pre0.append(("consts", consts_ld))
            items = [(ybf[par][:, dc * 128:(dc + 1) * 128],
                      y_fm[:, dc * TH + thc * 128: dc * TH + thc * 128 + 128])
                     for dc in range(DC)]
            r0 = tr_batch(items, [psT0, psT1, psE, psF], pre=pre0)
            reads(f"ybf{par}", "T", r0["lastT"])

        close_stack(es0)
        if upto <= 0:

            _dump_yfm(block, H, C, inc, y_fm, dbg)
            _finish(block, H, C)
            close_all()
            return nc
        barrier()

        es_hk = new_stack()
        wk_bf = sbuf("wk_bf", [128, DC * 1024], BF16, es_hk)
        hko = [sbuf(f"hko{i}", [128, 512], BF16, es_hk) for i in range(2)]

        def emit_wk_half(half):
            @block.gpsimd
            def _(g):
                guard(g, "wk_bf")
                for dc in range(DC):
                    inc(g.dma_start(wk_bf[:, dc * 1024:(dc + 1) * 1024],
                                    wk_nat[dc * 128:(dc + 1) * 128,
                                           half * 1024:(half + 1) * 1024]), "ld_wk", 16)
            return C["ld_wk"]

        wk_ld = [emit_wk_half(0), None]

        def emit_hk_group(it):
            jc = it // NTB
            tb = it % NTB
            jloc = jc % 8
            par = it % 2
            pp = psAB[par]

            @block.tensor
            def _(t):
                w(t, "ld_wk", wk_ld[jc // 8])
                guard(t, bname(pp))
                for dc in range(DC):
                    ins = t.matmul(pp[:],
                                   wk_bf[:, dc * 1024 + jloc * 128: dc * 1024 + jloc * 128 + 128],
                                   y_fm[:, dc * TH + HALO + tb * TBS: dc * TH + HALO + (tb + 1) * TBS],
                                   start=(dc == 0), stop=(dc == DC - 1))
                inc(ins, "T", 1)
                if it == 8 * NTB - 1:
                    reads("wk_bf", "T", C["T"])
            mm = C["T"]

            @block.vector
            def _(v):
                w(v, "T", mm)
                guard(v, f"hko{par}")
                inc(v.tensor_copy(hko[par][:], pp[:]), "V", 1)
                reads(bname(pp), "V", C["V"])
            ev = C["V"]

            @block.sync
            def _(y):
                y.wait_ge(H["V"], ev)
                inc(y.dma_start(S_hk[tb, jc], hko[par][:]), f"st_hko{par}", 16)
                reads(f"hko{par}", f"st_hko{par}", C[f"st_hko{par}"])

        hkq = []
        for it in range(8 * NTB):
            hkq.append(lambda it=it: emit_hk_group(it))
        hkq.append(lambda: wk_ld.__setitem__(1, emit_wk_half(1)))
        for it in range(8 * NTB, HK_JCS * NTB):
            hkq.append(lambda it=it: emit_hk_group(it))

        if upto <= 1:
            while hkq:
                hkq.pop(0)()
            close_stack(es_hk)
            _dump_yfm(block, H, C, inc, y_fm, dbg)
            _finish(block, H, C)
            close_all()
            return nc

        es2 = new_stack()
        wsl = {nm: [sbuf(f"wsl_{nm}{i}", [128, D], BF16, es2) for i in range(2)]
               for nm in ("q", "k", "v")}
        qf = [sbuf(f"qf{i}", [128, T], BF16, es2) for i in range(2)]
        kf = [sbuf(f"kf{i}", [128, TH], BF16, es2) for i in range(2)]
        vf = sbuf("vf", [128, TH], BF16, es2)
        vtm = [sbuf(f"vtm{i}", [128, TH], BF16, es2) for i in range(2)]
        p_e = [sbuf(f"p_e{i}", [128, 384], BF16, es2) for i in range(2)]
        p_m = [sbuf(f"p_m{i}", [128, 384], BF16, es2) for i in range(2)]
        at_bf = [sbuf(f"at_bf{i}", [128, 384], BF16, es2) for i in range(2)]
        ctxo = [sbuf(f"ctxo{i}", [128, 512], BF16, es2) for i in range(2)]
        rs1 = sbuf("rs1", [128, 4], F32, es2)

        wsl_ld = {}

        def head_srcs(h):
            return [(nm, src) for nm, src in zip(
                ("q", "k", "v"),
                [wqT_h[h].rearrange("(c p) o -> p c o", p=128),
                 wkT_h[h].rearrange("(c p) o -> p c o", p=128),
                 wvT_h[h].rearrange("(c p) o -> p c o", p=128)])]

        @block.gpsimd
        def _(g):
            for h in (0, 1):
                for nm, src in head_srcs(h):
                    guard(g, f"wsl_{nm}{h % 2}")
                    inc(g.dma_start(wsl[nm][h % 2][:].rearrange("p (c o) -> p c o", o=128), src),
                        f"ld_wsl{h % 2}", 16)
                wsl_ld[h] = C[f"ld_wsl{h % 2}"]

        qkv_ev = {}
        QKV_GROUPS = ([("q", tb) for tb in range(4)] +
                      [("k", tb) for tb in range(5)] +
                      [("v", tb) for tb in range(5)])
        WIDTHS = {"q": T, "k": TH, "v": TH}
        SRCOFF = {"q": HALO, "k": 0, "v": 0}

        def emit_qkv_group(h, gi):
            par = h % 2
            nm, tb = QKV_GROUPS[gi]
            dst = {"q": qf[par], "k": kf[par], "v": vf}[nm]
            pp = psEF[gi % 2]
            n = min(512, WIDTHS[nm] - tb * 512)

            @block.tensor
            def _(t):
                if gi == 0:
                    w(t, f"ld_wsl{par}", wsl_ld[h])
                    guard(t, f"qf{par}")
                    guard(t, f"kf{par}")
                if nm == "v" and tb == 0:
                    guard(t, "vf")
                guard(t, bname(pp))
                for dc in range(DC):
                    ins = t.matmul(pp[:, :n], wsl[nm][par][:, dc * 128:(dc + 1) * 128],
                                   y_fm[:, dc * TH + SRCOFF[nm] + tb * 512:
                                        dc * TH + SRCOFF[nm] + tb * 512 + n],
                                   start=(dc == 0), stop=(dc == DC - 1))
                inc(ins, "T", 1)
                if gi == len(QKV_GROUPS) - 1:
                    reads(f"wsl_q{par}", "T", C["T"])
                    reads(f"wsl_k{par}", "T", C["T"])
                    reads(f"wsl_v{par}", "T", C["T"])
            mm = C["T"]

            @block.vector
            def _(v):
                w(v, "T", mm)
                inc(v.tensor_copy(dst[:, tb * 512: tb * 512 + n], pp[:, :n]), "V", 1)
                reads(bname(pp), "V", C["V"])
            qkv_ev[(h, nm, tb)] = ("V", C["V"])

        def emit_attn_pair(h, qt0, fillers=()):
            par = h % 2
            qt1 = qt0 + 1
            fillers = list(fillers)

            @block.tensor
            def _(t):
                w(t, *qkv_ev[(h, "q", (qt1 * 128) // 512)])
                w(t, *qkv_ev[(h, "k", min((qt1 * 128 + 383) // 512, 4))])
                guard(t, bname(psA))
                t.matmul(psA[:, :384], qf[par][:, qt0 * 128:(qt0 + 1) * 128],
                         kf[par][:, qt0 * 128: qt0 * 128 + 384], start=True, stop=True)
                guard(t, bname(psB))
                ins = t.matmul(psB[:, :384], qf[par][:, qt1 * 128:(qt1 + 1) * 128],
                               kf[par][:, qt1 * 128: qt1 * 128 + 384], start=True, stop=True)
                inc(ins, "T", 1)
                if qt1 == NTC - 1:
                    reads(f"qf{par}", "T", C["T"])
                    reads(f"kf{par}", "T", C["T"])
            sc = C["T"]
            if fillers and fillers[0][0] == "qkv":
                fillers.pop(0)[1]()

            @block.scalar
            def _(s):
                w(s, "T", sc)
                guard(s, "p_e0")
                guard(s, "p_e1")
                s.activation(p_e[0][:], psA[:, :384], ACT.Exp, scale=ISQH)
                ins = s.activation(p_e[1][:], psB[:, :384], ACT.Exp, scale=ISQH)
                inc(ins, "C", 1)
                reads(bname(psA), "C", C["C"])
                reads(bname(psB), "C", C["C"])
            ex = C["C"]

            mask0 = masks_bf[:, (qt0 if qt0 < 2 else 2) * 384:(qt0 if qt0 < 2 else 2) * 384 + 384]
            mask1 = masks_bf[:, (qt1 if qt1 < 2 else 2) * 384:(qt1 if qt1 < 2 else 2) * 384 + 384]

            @block.vector
            def _(v):
                w(v, "C", ex)
                guard(v, "p_m0")
                guard(v, "p_m1")
                v.scalar_tensor_tensor(p_m[0][:], p_e[0][:], 0.0, mask0,
                                       op0=ALU.add, op1=ALU.mult, accum_out=rs1[:, 0:1])
                vchain(v, v.scalar_tensor_tensor(p_m[1][:], p_e[1][:], 0.0, mask1,
                                                 op0=ALU.add, op1=ALU.mult,
                                                 accum_out=rs1[:, 2:3]))
                v.reciprocal(rs1[:, 1:2], rs1[:, 0:1])
                vchain(v, v.reciprocal(rs1[:, 3:4], rs1[:, 2:3]))
                v.tensor_scalar_mul(p_m[0][:], p_m[0][:], rs1[:, 1:2])
                ins = v.tensor_scalar_mul(p_m[1][:], p_m[1][:], rs1[:, 3:4])
                inc(ins, "V", 1)
                reads("p_e0", "V", C["V"])
                reads("p_e1", "V", C["V"])
            pn = C["V"]

            items = [(p_m[0][:, c * 128:(c + 1) * 128], at_bf[0][:, c * 128:(c + 1) * 128])
                     for c in range(3)] +                     [(p_m[1][:, c * 128:(c + 1) * 128], at_bf[1][:, c * 128:(c + 1) * 128])
                     for c in range(3)]
            ra = tr_batch(items, [psT0, psT1], pre=[("V", pn)],
                          dst_guards=["at_bf0", "at_bf1"])
            reads("p_m0", "T", ra["lastT"])
            reads("p_m1", "T", ra["lastT"])
            while fillers:
                fillers.pop(0)[1]()

            @block.tensor
            def _(t):
                for s_, v_ in ra["done"].items():
                    w(t, s_, v_)
                guard(t, bname(psC))
                for c in range(3):
                    t.matmul(psC[:, :128], vtm[par][:, (qt0 + c) * 128:(qt0 + c + 1) * 128],
                             at_bf[0][:, c * 128:(c + 1) * 128],
                             start=(c == 0), stop=(c == 2))
                guard(t, bname(psD))
                for c in range(3):
                    ins = t.matmul(psD[:, :128], vtm[par][:, (qt1 + c) * 128:(qt1 + c + 1) * 128],
                                   at_bf[1][:, c * 128:(c + 1) * 128],
                                   start=(c == 0), stop=(c == 2))
                inc(ins, "T", 1)
                reads("at_bf0", "T", C["T"])
                reads("at_bf1", "T", C["T"])
                if qt1 >= NTC - 2:
                    reads(f"vtm{par}", "T", C["T"])
            cm = C["T"]

            tb = qt0 // 4
            tpar = tb % 2

            @block.vector
            def _(v):
                w(v, "T", cm)
                if qt0 % 4 == 0:
                    guard(v, f"ctxo{tpar}")
                v.tensor_copy(ctxo[tpar][:, (qt0 % 4) * 128:(qt0 % 4) * 128 + 128], psC[:, :128])
                ins = v.tensor_copy(ctxo[tpar][:, (qt1 % 4) * 128:(qt1 % 4) * 128 + 128],
                                    psD[:, :128])
                inc(ins, "V", 1)
                reads(bname(psC), "V", C["V"])
                reads(bname(psD), "V", C["V"])
            ce = C["V"]

            if qt1 % 4 == 3:
                @block.sync
                def _(y):
                    y.wait_ge(H["V"], ce)
                    inc(y.dma_start(S_ctx[tb, h], ctxo[tpar][:]), f"st_ctxo{tpar}", 16)
                    reads(f"ctxo{tpar}", f"st_ctxo{tpar}", C[f"st_ctxo{tpar}"])
            pc_pump()

        for gi in range(len(QKV_GROUPS)):
            emit_qkv_group(0, gi)

        for h in range(NH):
            par = h % 2

            if h + 2 < NH:
                @block.gpsimd
                def _(g):
                    for nm, src in head_srcs(h + 2):
                        guard(g, f"wsl_{nm}{par}")
                        inc(g.dma_start(wsl[nm][par][:].rearrange("p (c o) -> p c o", o=128), src),
                            f"ld_wsl{par}", 16)
                    wsl_ld[h + 2] = C[f"ld_wsl{par}"]

            items = [(vf[:, kc * 128:(kc + 1) * 128], vtm[par][:, kc * 128:(kc + 1) * 128])
                     for kc in range(NTHC)]
            rv = tr_batch(items, [psT0, psT1],
                          pre=[qkv_ev[(h, "v", 4)]], dst_guards=[f"vtm{par}"])
            reads("vf", "T", rv["lastT"])

            nxt = list(range(len(QKV_GROUPS))) if h + 1 < NH else []
            for qp in range(NTC // 2):
                fill = []
                for _ in range(2):
                    if nxt:
                        gi = nxt.pop(0)
                        fill.append(("qkv", lambda h1=h + 1, gi=gi: emit_qkv_group(h1, gi)))
                for _ in range(2):
                    if hkq:
                        fill.append(("hk", hkq.pop(0)))
                emit_attn_pair(h, 2 * qp, fill)

        while hkq:
            hkq.pop(0)()
        pc_drain()
        _dump_yfm(block, H, C, inc, y_fm, dbg)
        close_stack(es2)
        close_stack(es_hk)
        close_stack(es_y)
        close_stack(es_pc)
        if upto <= 2:
            _finish(block, H, C)
            close_all()
            return nc
        barrier()

        tb_snap = {"w": None}
        for tb in range(NTB):
            es5o = new_stack()
            es5a = new_stack()
            w_T = sbuf(f"w_T_{tb}", [128, 32 * 512], BF16, es5o)
            p1_fm = sbuf(f"p1_fm_{tb}", [128, DC * 512], BF16, es5o)
            p2_fm = sbuf(f"p2_fm_{tb}", [128, DC * 512], BF16, es5o)
            hk_sl = sbuf(f"hk_sl_{tb}", [128, DC * TBS], BF16, es5a)
            sim_sb = [sbuf(f"sim{i}_{tb}", [128, M], F32, es5a) for i in range(4)]
            es5m = new_stack()
            mbT_sl = [sbuf(f"mbT_sl{i}_{tb}", [128, DC * 512], BF16, es5m) for i in range(2)]

            @block.gpsimd
            def _(g):
                if tb_snap["w"]:
                    for s_, v_ in tb_snap["w"]:
                        w(g, s_, v_)
                guard(g, "hk_sl")
                inc(g.dma_start(hk_sl[:].rearrange("p (c n) -> p c n", n=TBS),
                                S_hk[tb].rearrange("c p n -> p c n")), "ld_hksl", 16)
            hk_ld = C["ld_hksl"]

            sim_ev = {}
            for mb in range(8):
                par = mb % 2

                @block.gpsimd
                def _(g):
                    guard(g, f"mbT_sl{par}")
                    inc(g.dma_start(mbT_sl[par][:].rearrange("p (c n) -> p c n", n=512),
                                    B_mbT[mb]), f"ld_mbT{par}", 16)
                mbT_ld = (f"ld_mbT{par}", C[f"ld_mbT{par}"])

                for tsub in range(4):
                    pp = psAB[tsub % 2]

                    @block.tensor
                    def _(t):
                        if mb == 0 and tsub == 0:
                            w(t, "ld_hksl", hk_ld)
                        w(t, mbT_ld[0], mbT_ld[1])
                        guard(t, bname(pp))
                        for jc in range(DC):
                            ins = t.matmul(pp[:],
                                     hk_sl[:, jc * TBS + tsub * 128: jc * TBS + tsub * 128 + 128],
                                     mbT_sl[par][:, jc * 512:(jc + 1) * 512],
                                     start=(jc == 0), stop=(jc == DC - 1))
                        inc(ins, "T", 1)
                        if tsub == 3:
                            reads(f"mbT_sl{par}", "T", C["T"])
                        if mb == 7 and tsub == 3:
                            reads("hk_sl", "T", C["T"])
                    mm = C["T"]

                    @block.vector
                    def _(v):
                        w(v, "T", mm)
                        guard(v, f"sim{tsub}_{mb}")
                        inc(v.tensor_scalar_mul(sim_sb[tsub][:, mb * 512:(mb + 1) * 512],
                                                pp[:], ISQD), "V", 1)
                        reads(bname(pp), "V", C["V"])
                    sim_ev[(mb, tsub)] = C["V"]

            if debug and tb == 0:
                @block.sync
                def _(y):
                    for tsub in range(4):
                        y.wait_ge(H["V"], sim_ev[(7, tsub)])
                        inc(y.dma_start(dbg["sim0"][tsub], sim_sb[tsub][:]), "st_dbg", 16)
                        for _mb in range(8):
                            reads(f"sim{tsub}_{_mb}", "st_dbg", C["st_dbg"])

            close_stack(es5m)
            mfree = {"T": C["T"], "V": C["V"]}
            es5e = new_stack()
            e_f = sbuf(f"e_f_{tb}", [128, M], F32, es5e)
            w_bf = sbuf(f"w_bf_{tb}", [128, M], BF16, es5e)

            wtr_cp = {}
            for tsub in range(4):
                tc = tb * 4 + tsub

                @block.vector
                def _(v):
                    w(v, "V", sim_ev[(7, tsub)])
                    guard(v, "top8")
                    inc(v.max(out=top8[:], in_=sim_sb[tsub][:]), "V", 1)
                t8_done = C["V"]

                @block.scalar
                def _(s):
                    w(s, "V", sim_ev[(7, tsub)])
                    if tsub == 0:
                        for s_, v_ in mfree.items():
                            w(s, s_, v_)
                    guard(s, "e_f")
                    inc(s.activation(e_f[:], sim_sb[tsub][:], ACT.Exp), "C", 1)
                ee_done = C["C"]

                @block.vector
                def _(v):
                    w(v, "C", ee_done)
                    w(v, "V", t8_done)
                    guard(v, "w_bf")
                    ins = v.scalar_tensor_tensor(w_bf[:], sim_sb[tsub][:], top8[:, 7:8], e_f[:],
                                                 op0=ALU.is_ge, op1=ALU.mult,
                                                 accum_out=rowsums[:, tc:tc + 1])
                    inc(ins, "V", 1)
                    reads("e_f", "V", C["V"])
                    reads("top8", "V", C["V"])
                    for _mb in range(8):
                        reads(f"sim{tsub}_{_mb}", "V", C["V"])
                stt_done = C["V"]

                items = [(w_bf[:, mc * 128:(mc + 1) * 128],
                          w_T[:, mc * 512 + tsub * 128: mc * 512 + tsub * 128 + 128])
                         for mc in range(32)]
                rw = tr_batch(items, [psT0, psT1, psC, psD],
                              pre=[("V", stt_done)],
                              dst_guards=(["w_T"] if tsub == 0 else None))
                reads("w_bf", "T", rw["lastT"])
                wtr_cp[tsub] = dict(rw["done"])

            close_stack(es5e)
            close_stack(es5a)
            free_snap = {"T": C["T"], "V": C["V"], "C": C["C"]}
            es5b = new_stack()
            mb_sl = [sbuf(f"mb_sl{i}_{tb}", [128, M], BF16, es5b) for i in range(2)]
            wv_sl = [sbuf(f"wv_sl{i}_{tb}", [128, D], BF16, es5b) for i in range(2)]

            p1_ev = {}
            for dc in range(DC):
                par = dc % 2

                @block.gpsimd
                def _(g):
                    if dc in (0, 1):
                        for s_, v_ in free_snap.items():
                            w(g, s_, v_)
                    guard(g, f"mb_sl{par}")
                    inc(g.dma_start(mb_sl[par][:].rearrange("p (a c) -> p a c", c=128),
                                    B_mb[dc]), f"ld_mb{par}", 16)
                mb_ld = (f"ld_mb{par}", C[f"ld_mb{par}"])

                pp = psAB[par]

                @block.tensor
                def _(t):
                    w(t, mb_ld[0], mb_ld[1])
                    if dc == 0:
                        for s_, v_ in wtr_cp[3].items():
                            w(t, s_, v_)
                        guard(t, "p1_full")
                    guard(t, bname(pp))
                    for mc in range(32):
                        ins = t.matmul(pp[:], mb_sl[par][:, mc * 128:(mc + 1) * 128],
                                 w_T[:, mc * 512:(mc + 1) * 512],
                                 start=(mc == 0), stop=(mc == 31))
                    inc(ins, "T", 1)
                    reads(f"mb_sl{par}", "T", C["T"])
                    if dc == DC - 1:
                        reads("w_T", "T", C["T"])
                mm = C["T"]

                @block.vector
                def _(v):
                    w(v, "T", mm)
                    inc(v.tensor_copy(p1_fm[:, dc * 512:(dc + 1) * 512], pp[:]), "V", 1)
                    reads(bname(pp), "V", C["V"])
                p1_ev[dc] = C["V"]

            @block.tensor
            def _(t):
                w(t, "V", ones_done)
                for gg, vv in wtr_cp[3].items():
                    w(t, gg, vv)
                guard(t, bname(psF))
                for mc in range(32):
                    ins = t.matmul(psF[0:1, 0:512], ones_bf[:], w_T[:, mc * 512:(mc + 1) * 512],
                                   start=(mc == 0), stop=(mc == 31))
                inc(ins, "T", 1)
            rsr_mm = C["T"]

            @block.vector
            def _(v):
                w(v, "T", rsr_mm)
                guard(v, "rs_row")
                inc(v.tensor_copy(rs_row[0:1, :], psF[0:1, 0:512]), "V", 1)
                reads(bname(psF), "V", C["V"])
            rsr_ev = C["V"]

            p2_ev = {}
            for d2c in range(DC):
                par = d2c % 2

                @block.gpsimd
                def _(g):
                    guard(g, f"wv_sl{par}")
                    inc(g.dma_start(wv_sl[par][:].rearrange("p (c o) -> p c o", o=128),
                                    B_wv[d2c]), f"ld_wv{par}", 16)
                wv_ld = (f"ld_wv{par}", C[f"ld_wv{par}"])

                pp = psCD[par]

                @block.tensor
                def _(t):
                    w(t, wv_ld[0], wv_ld[1])
                    w(t, "V", p1_ev[DC - 1])
                    if d2c == 0:
                        guard(t, "p2_full")
                    guard(t, bname(pp))
                    for dc in range(DC):
                        ins = t.matmul(pp[:], wv_sl[par][:, dc * 128:(dc + 1) * 128],
                                 p1_fm[:, dc * 512:(dc + 1) * 512],
                                 start=(dc == 0), stop=(dc == DC - 1))
                    inc(ins, "T", 1)
                    reads(f"wv_sl{par}", "T", C["T"])
                    if d2c == DC - 1:
                        reads("p1_full", "T", C["T"])
                mm = C["T"]

                @block.vector
                def _(v):
                    w(v, "T", mm)
                    inc(v.tensor_copy(p2_fm[:, d2c * 512:(d2c + 1) * 512], pp[:]), "V", 1)
                    reads(bname(pp), "V", C["V"])
                p2_ev[d2c] = C["V"]

            if debug and tb == 0:
                @block.sync
                def _(y):
                    y.wait_ge(H["V"], p1_ev[DC - 1])
                    inc(y.dma_start(dbg["p1_0"].rearrange("c p n -> p c n"),
                                    p1_fm[:].rearrange("p (c n) -> p c n", n=512)), "st_dbg", 16)
                    reads("p1_full", "st_dbg", C["st_dbg"])
                    y.wait_ge(H["V"], p2_ev[DC - 1])
                    inc(y.dma_start(dbg["p2_0"].rearrange("c p n -> p c n"),
                                    p2_fm[:].rearrange("p (c n) -> p c n", n=512)), "st_dbg", 16)
                    reads("p2_full", "st_dbg", C["st_dbg"])

            woT_bf = sbuf(f"woT_bf_{tb}", [128, DC * D], BF16, es5b)
            ctx_tb = sbuf(f"ctx_tb_{tb}", [128, DC * 512], BF16, es5b)
            ctxp = sbuf(f"ctxp_{tb}", [128, DC * 512], BF16, es5b)
            sbc_bf = sbuf(f"sbc_bf_{tb}", [128, 512], BF16, es5b)
            ao_bf = sbuf(f"ao_bf_{tb}", [128, D], BF16, es5b)

            @block.gpsimd
            def _(g):
                guard(g, "woT_bf")
                for ic in range(DC):
                    inc(g.dma_start(woT_bf[:, ic * D:(ic + 1) * D], B_wo[ic]), "ld_woT", 16)
                guard(g, "ctx_tb")
                inc(g.dma_start(ctx_tb[:].rearrange("p (c n) -> p c n", n=512),
                                S_ctx[tb].rearrange("c p n -> p c n")), "ld_ctxtb", 16)
            wo_ld = C["ld_woT"]
            ctx_ld = C["ld_ctxtb"]

            if tb == 0:
                @block.vector
                def _(v):
                    w(v, "ld_woT", wo_ld)
                    guard(v, "ao_bf")
                    for ic in range(DC):
                        vchain(v, v.tensor_mul(ao_bf[:], woT_bf[:, ic * D:(ic + 1) * D],
                                               gw_bf[:]))
                        vchain(v, v.reduce_sum(gv_sb[:, ic:ic + 1], ao_bf[:], axis=AX))
                    inc(v.tensor_copy(gv_bf[:], gv_sb[:]), "V", 1)
                    reads("ao_bf", "V", C["V"])
                gv_state["done"] = C["V"]
            gv_done = gv_state["done"]

            for tsub in range(4):
                pe = psEF[tsub % 2]

                @block.tensor
                def _(t):
                    if tsub == 0:
                        w(t, "V", gv_done)
                        w(t, "ld_ctxtb", ctx_ld)
                    guard(t, bname(pe))
                    for ic in range(DC):
                        ins = t.matmul(pe[0:1, 0:128], gv_bf[:, ic:ic + 1],
                                       ctx_tb[:, ic * 512 + tsub * 128: ic * 512 + tsub * 128 + 128],
                                       start=(ic == 0), stop=(ic == DC - 1))
                    inc(ins, "T", 1)
                bmm = C["T"]

                @block.vector
                def _(v):
                    w(v, "T", bmm)
                    if tsub == 0:
                        guard(v, "bp_row")
                    inc(v.tensor_copy(bp_row[0:1, tsub * 128:(tsub + 1) * 128], pe[0:1, 0:128]),
                        "V", 1)
                    reads(bname(pe), "V", C["V"])
            bp_done = C["V"]

            @block.vector
            def _(v):
                w(v, "V", bp_done)
                w(v, "V", rsr_ev)
                vchain(v, v.tensor_tensor(bp_row[0:1, :], bp_row[0:1, :],
                                          gb_sb[0:1, 0:1].to_broadcast((1, 512)), op=ALU.add))
                inc(v.reciprocal(rs_row[0:1, :], rs_row[0:1, :]), "V", 1)
            sg0 = C["V"]

            @block.scalar
            def _(s):
                w(s, "V", sg0)
                guard(s, "scl_row")
                inc(s.activation(scl_row[0:1, :], bp_row[0:1, :], ACT.Exp, scale=-1.0), "C", 1)
                reads("bp_row", "C", C["C"])
            sg1 = C["C"]

            @block.vector
            def _(v):
                w(v, "C", sg1)
                vchain(v, v.tensor_scalar_add(scl_row[0:1, :], scl_row[0:1, :], 1.0))
                vchain(v, v.reciprocal(scl_row[0:1, :], scl_row[0:1, :]))
                vchain(v, v.tensor_tensor(scl_row[0:1, :], scl_row[0:1, :], rs_row[0:1, :],
                                          op=ALU.mult))
                inc(v.tensor_copy(sclb_row[0:1, :], scl_row[0:1, :]), "V", 1)
                reads("scl_row", "V", C["V"])
                reads("rs_row", "V", C["V"])
            sg2 = C["V"]

            @block.tensor
            def _(t):
                w(t, "V", sg2)
                w(t, "V", ones_done)
                guard(t, bname(psE))
                ins = t.matmul(psE[:, :512], onesr_bf[:], sclb_row[0:1, :],
                               start=True, stop=True)
                inc(ins, "T", 1)
                reads("sclb_row", "T", C["T"])
            sbc = C["T"]

            @block.vector
            def _(v):
                w(v, "T", sbc)
                w(v, "V", p2_ev[DC - 1])
                guard(v, "ctxp")
                guard(v, "tmp5")
                guard(v, "sbc_bf")
                vchain(v, v.tensor_copy(sbc_bf[:], psE[:, :512]))
                reads(bname(psE), "V", C["V"])
                for ic in range(DC):
                    ins = v.tensor_tensor(ctxp[:, ic * 512:(ic + 1) * 512],
                                          p2_fm[:, ic * 512:(ic + 1) * 512], sbc_bf[:],
                                          op=ALU.mult)
                vchain(v, ins)
                for ic in range(DC):
                    ins = v.tensor_tensor(ctxp[:, ic * 512:(ic + 1) * 512],
                                          ctxp[:, ic * 512:(ic + 1) * 512],
                                          ctx_tb[:, ic * 512:(ic + 1) * 512], op=ALU.add)
                inc(ins, "V", 1)
                reads("sbc_bf", "V", C["V"])
                reads("p2_full", "V", C["V"])
                reads("ctx_tb", "V", C["V"])
            ctxp_done = C["V"]

            for tsub in range(4):
                for ob in range(4):
                    pp = psAB[ob % 2]

                    @block.tensor
                    def _(t):
                        if tsub == 0 and ob == 0:
                            w(t, "ld_woT", wo_ld)
                            w(t, "V", ctxp_done)
                        guard(t, bname(pp))
                        for ic in range(DC):
                            ins = t.matmul(pp[:],
                                           ctxp[:, ic * 512 + tsub * 128: ic * 512 + tsub * 128 + 128],
                                           woT_bf[:, ic * D + ob * 512: ic * D + (ob + 1) * 512],
                                           start=(ic == 0), stop=(ic == DC - 1))
                        inc(ins, "T", 1)
                        if tsub == 3 and ob == 3:
                            reads("ctxp", "T", C["T"])
                            reads("woT_bf", "T", C["T"])
                    mm = C["T"]

                    @block.vector
                    def _(v):
                        w(v, "T", mm)
                        if ob == 0:
                            guard(v, "ao_bf")
                        inc(v.tensor_copy(ao_bf[:, ob * 512:(ob + 1) * 512], pp[:]), "V", 1)
                        reads(bname(pp), "V", C["V"])
                    comb3 = C["V"]

                @block.sync
                def _(y):
                    y.wait_ge(H["V"], comb3)
                    inc(y.dma_start(S_ao[tb, tsub], ao_bf[:]), "st_ao", 16)
                    reads("ao_bf", "st_ao", C["st_ao"])

            close_stack(es5b)
            close_stack(es5o)
            tb_snap["w"] = [("T", C["T"]), ("V", C["V"]), ("st_ao", C["st_ao"])]

        barrier()
        if upto <= 3:
            _finish(block, H, C)
            close_all()
            return nc

        es67 = new_stack()
        y2_fm = sbuf("y2_fm", [128, DC * T], BF16)
        W1_bf = sbuf("W1_bf", [128, DC * D], BF16, es67)
        h_bf = sbuf("h_bf", [128, DC * 512], BF16, es67)
        lg8 = sbuf("lg8", [128, EXP], F32, es67)
        ee8 = sbuf("ee8", [128, EXP], F32, es67)
        nmx = sbuf("nmx", [128, 2], F32, es67)
        xt6 = [sbuf(f"xt6{i}", [128, D], F32, es67) for i in range(2)]
        ao6 = [sbuf(f"ao6{i}", [128, D], BF16, es67) for i in range(2)]
        x1t = sbuf("x1t", [128, D], F32, es67)
        x1b = sbuf("x1b", [128, D], BF16, es67)
        y2b = [sbuf(f"y2b{i}", [128, D], BF16, es67) for i in range(2)]

        @block.gpsimd
        def _(g):
            for dc in range(DC):
                inc(g.dma_start(W1_bf[:, dc * D:(dc + 1) * D],
                                W1_nat[dc * 128:(dc + 1) * 128, :]), "ld_w1", 16)
        w1_ld = C["ld_w1"]

        def emit_ln2(tc):
            par = tc % 2
            tb, tsub = tc // 4, tc % 4

            @block.gpsimd
            def _(g):
                guard(g, f"xt6{par}")
                guard(g, f"ao6{par}")
                inc(g.dma_start(xt6[par][:], x_in[HALO + tc * 128: HALO + (tc + 1) * 128, :]),
                    f"ld_p6{par}", 16)
                inc(g.dma_start(ao6[par][:], S_ao[tb, tsub]), f"ld_p6{par}", 16)
            x6_ld = C[f"ld_p6{par}"]

            @block.vector
            def _(v):
                w(v, f"ld_p6{par}", x6_ld)
                guard(v, "x1t")
                guard(v, "x1b")
                vchain(v, v.tensor_tensor(x1t[:], xt6[par][:], ao6[par][:], op=ALU.add))
                v.tensor_copy(x1b[:], x1t[:])
                vchain(v, v.reduce_sum(stat[:, 0:1], x1t[:], axis=AX))
                inc(v.tensor_scalar_mul(stat[:, 1:2], stat[:, 0:1], 1.0 / D), "V", 1)
                reads(f"xt6{par}", "V", C["V"])
                reads(f"ao6{par}", "V", C["V"])
            m6 = C["V"]

            @block.scalar
            def _(s):
                w(s, "V", m6)
                guard(s, "st23")
                inc(s.activation(xt6[par][:], x1t[:], ACT.Square, accum_out=stat[:, 2:3]), "C", 1)
                reads(f"xt6{par}", "C", C["C"])
            sq6 = C["C"]

            @block.vector
            def _(v):
                w(v, "C", sq6)
                w(v, "V", m6)
                vchain(v, v.tensor_scalar(stat[:, 4:5], stat[:, 1:2], stat[:, 1:2], EPS,
                                          op0=ALU.mult, op1=ALU.subtract))
                inc(v.tensor_scalar(stat[:, 5:6], stat[:, 2:3], 1.0 / D, stat[:, 4:5],
                                    op0=ALU.mult, op1=ALU.subtract), "V", 1)
                reads("st23", "V", C["V"])
            v6 = C["V"]

            @block.scalar
            def _(s):
                w(s, "V", v6)
                guard(s, "st67")
                inc(s.activation(stat[:, 6:7], stat[:, 5:6], ACT.Sqrt), "C", 1)
            sd6 = C["C"]

            @block.vector
            def _(v):
                w(v, "C", sd6)
                guard(v, f"y2b{par}")
                vchain(v, v.reciprocal(stat[:, 7:8], stat[:, 6:7]))
                ins = v.tensor_scalar(y2b[par][:], x1t[:], stat[:, 1:2], stat[:, 7:8],
                                      op0=ALU.subtract, op1=ALU.mult)
                inc(ins, "V", 1)
                reads("x1t", "V", C["V"])
                reads("st67", "V", C["V"])
            n6 = C["V"]

            @block.sync
            def _(y):
                y.wait_ge(H["V"], m6)
                inc(y.dma_start(S_x1[tc], x1b[:]), "st_x1", 16)
                reads("x1b", "st_x1", C["st_x1"])

            items = [(y2b[par][:, dc * 128:(dc + 1) * 128],
                      y2_fm[:, dc * T + tc * 128: dc * T + tc * 128 + 128])
                     for dc in range(DC)]
            r6 = tr_batch(items, [psT0, psT1], pre=[("V", n6)])
            reads(f"y2b{par}", "T", r6["lastT"])

        def emit_h_block(tb, y2v):
            h_ev = {}
            for ec in range(DC):
                pp = psAB[ec % 2]

                @block.tensor
                def _(t):
                    if ec == 0:
                        w(t, "V", y2v)
                        if tb == 0:
                            w(t, "ld_w1", w1_ld)
                    guard(t, bname(pp))
                    guard(t, f"h{ec}")
                    for dc in range(DC):
                        ins = t.matmul(pp[:],
                                       W1_bf[:, dc * D + ec * 128: dc * D + ec * 128 + 128],
                                       y2_fm[:, dc * T + tb * TBS: dc * T + (tb + 1) * TBS],
                                       start=(dc == 0), stop=(dc == DC - 1))
                    inc(ins, "T", 1)
                mm = C["T"]

                @block.vector
                def _(v):
                    w(v, "T", mm)
                    ins = v.tensor_scalar(h_bf[:, ec * 512:(ec + 1) * 512], pp[:],
                                          b1_sb[:, ec:ec + 1], 0.0,
                                          op0=ALU.add, op1=ALU.max)
                    inc(ins, "V", 1)
                    reads(bname(pp), "V", C["V"])
                h_ev[ec] = C["V"]
            return h_ev

        def emit_logits(tb, h_ev):
            for tsub in range(4):
                tc = tb * 4 + tsub
                pp = psEF[tsub % 2]

                @block.tensor
                def _(t):
                    w(t, "V", h_ev[DC - 1])
                    guard(t, bname(pp))
                    for ec in range(DC):
                        ins = t.matmul(pp[:, :EXP],
                                       h_bf[:, ec * 512 + tsub * 128: ec * 512 + tsub * 128 + 128],
                                       W2_bf[:, ec * EXP:(ec + 1) * EXP],
                                       start=(ec == 0), stop=(ec == DC - 1))
                    inc(ins, "T", 1)
                    if tsub == 3:
                        for ec in range(DC):
                            reads(f"h{ec}", "T", C["T"])
                lgm = C["T"]

                @block.vector
                def _(v):
                    w(v, "T", lgm)
                    guard(v, "lg8")
                    guard(v, "nmx0")
                    vchain(v, v.tensor_tensor(lg8[:], pp[:, :EXP], b2_sb[:], op=ALU.add))
                    inc(v.reduce_max(nmx[:, 0:1], lg8[:], axis=AX, negate=True), "V", 1)
                    reads(bname(pp), "V", C["V"])
                lgv = C["V"]

                @block.scalar
                def _(s):
                    w(s, "V", lgv)
                    guard(s, "ee8")
                    inc(s.activation(ee8[:], lg8[:], ACT.Exp, bias=nmx[:, 0:1]), "C", 1)
                    reads("lg8", "C", C["C"])
                    reads("nmx0", "C", C["C"])
                eev = C["C"]

                @block.vector
                def _(v):
                    w(v, "C", eev)
                    vchain(v, v.reduce_sum(nmx[:, 1:2], ee8[:], axis=AX))
                    vchain(v, v.reciprocal(nmx[:, 1:2], nmx[:, 1:2]))
                    ins = v.tensor_scalar_mul(alpha_sb[:, tc * EXP:(tc + 1) * EXP],
                                              ee8[:], nmx[:, 1:2])
                    inc(ins, "V", 1)
                    reads("ee8", "V", C["V"])

        prev = None
        for tb in range(NTB):
            for tsub in range(4):
                emit_ln2(tb * 4 + tsub)
            y2v = C["V"]
            if prev is not None:
                emit_logits(prev[0], prev[1])
            h_ev = emit_h_block(tb, y2v)
            prev = (tb, h_ev)
        emit_logits(prev[0], prev[1])

        close_stack(es67)
        if debug:
            @block.sync
            def _(y):
                y.wait_ge(H["V"], C["V"])
                inc(y.dma_start(dbg["y2fm"].rearrange("c p n -> p c n"),
                                y2_fm[:].rearrange("p (c n) -> p c n", n=T)), "st_dbg", 16)
                inc(y.dma_start(dbg["alpha"], alpha_sb[:]), "st_dbg", 16)
                inc(y.dma_start(dbg["rs"], rowsums[:]), "st_dbg", 16)
        if upto <= 5:
            _finish(block, H, C)
            close_all()
            return nc
        barrier()

        es8 = new_stack()
        VT_bf = sbuf("VT_bf", [128, DC * 512], BF16, es8)
        UT_bf = sbuf("UT_bf", [128, 4 * D], BF16, es8)
        avx_bf = sbuf("avx_bf", [128, 512], BF16, es8)
        avx_fm = sbuf("avx_fm", [128, 512], BF16, es8)
        x1in = [sbuf(f"x1in{i}", [128, D], BF16, es8) for i in range(2)]
        outt = [sbuf(f"outt{i}", [128, D], F32, es8) for i in range(2)]

        @block.gpsimd
        def _(g):
            for dc in range(DC):
                inc(g.dma_start(VT_bf[:, dc * 512:(dc + 1) * 512],
                                VT_in[dc * 128:(dc + 1) * 128, :]), "ld_vut", 16)
            for krc in range(4):
                inc(g.dma_start(UT_bf[:, krc * D:(krc + 1) * D],
                                UT_in[krc * 128:(krc + 1) * 128, :]), "ld_vut", 16)
        vu_ld = C["ld_vut"]

        for tc in range(NTC):
            par = tc % 2
            pv = psAB[tc % 2]

            @block.gpsimd
            def _(g):
                guard(g, f"x1in{par}")
                inc(g.dma_start(x1in[par][:], S_x1[tc]), f"ld_x1in{par}", 16)
            x1_ld = C[f"ld_x1in{par}"]

            @block.tensor
            def _(t):
                if tc == 0:
                    w(t, "ld_vut", vu_ld)
                guard(t, bname(pv))
                for dc in range(DC):
                    ins = t.matmul(pv[:], y2_fm[:, dc * T + tc * 128: dc * T + tc * 128 + 128],
                             VT_bf[:, dc * 512:(dc + 1) * 512],
                             start=(dc == 0), stop=(dc == DC - 1))
                inc(ins, "T", 1)
            vxm = C["T"]

            @block.vector
            def _(v):
                w(v, "T", vxm)
                guard(v, "avx_bf")
                for k in range(EXP):
                    ins = v.tensor_scalar_mul(avx_bf[:, k * R:(k + 1) * R],
                                              pv[:, k * R:(k + 1) * R],
                                              alpha_sb[:, tc * EXP + k: tc * EXP + k + 1])
                inc(ins, "V", 1)
                reads(bname(pv), "V", C["V"])
            avxd = C["V"]

            items = [(avx_bf[:, c * 128:(c + 1) * 128],
                      avx_fm[:, c * 128:(c + 1) * 128]) for c in range(4)]
            r8 = tr_batch(items, [psT0, psT1],
                          pre=[("V", avxd)], dst_guards=["avx_fm"])
            reads("avx_bf", "T", r8["lastT"])
            avcp8 = dict(r8["done"])

            out_done = {}
            for db in range(4):
                pp = psCD[db % 2]

                @block.tensor
                def _(t):
                    for s_, v_ in avcp8.items():
                        w(t, s_, v_)
                    guard(t, bname(pp))
                    for krc in range(4):
                        ins = t.matmul(pp[:], avx_fm[:, krc * 128:(krc + 1) * 128],
                                 UT_bf[:, krc * D + db * 512: krc * D + (db + 1) * 512],
                                 start=(krc == 0), stop=(krc == 3))
                    inc(ins, "T", 1)
                    if db == 3:
                        reads("avx_fm", "T", C["T"])
                ofm = C["T"]

                @block.vector
                def _(v):
                    w(v, "T", ofm)
                    w(v, f"ld_x1in{par}", x1_ld)
                    guard(v, f"outt{par}_{db}")
                    ins = v.tensor_tensor(outt[par][:, db * 512:(db + 1) * 512], pp[:],
                                          x1in[par][:, db * 512:(db + 1) * 512], op=ALU.add)
                    inc(ins, "V", 1)
                    reads(bname(pp), "V", C["V"])
                    if db == 3:
                        reads(f"x1in{par}", "V", C["V"])
                out_done[db] = C["V"]

            @block.sync
            def _(y):
                y.wait_ge(H["V"], out_done[3])
                inc(y.dma_start(x_out[tc * 128:(tc + 1) * 128, :], outt[par][:]), "st_out", 16)
                for db in range(4):
                    reads(f"outt{par}_{db}", "st_out", C["st_out"])

        close_stack(es8)
        _finish(block, H, C)
        close_all()
    return nc


def _dump_yfm(block, H, C, inc, y_fm, dbg):
    if "yfm" not in dbg:
        return

    @block.sync
    def _(y):
        y.wait_ge(H["V"], C["V"])
        y.wait_ge(H["C"], C["C"])
        inc(y.dma_start(dbg["yfm"].rearrange("c p n -> p c n"),
                        y_fm[:].rearrange("p (c n) -> p c n", n=TH)), "st_dbg", 16)


def _finish(block, H, C):
    @block.sync
    def _(y):
        for s, v in C.items():
            if v > 0:
                y.wait_ge(H[s], v)


def _prep_inputs(inputs):
    f = np.float32
    hs = np.asarray(inputs["L0_attn_hidden_states"], dtype=f)
    wq = np.asarray(inputs["L0_attn_wq"], dtype=f)
    wk = np.asarray(inputs["L0_attn_wk"], dtype=f)
    wv = np.asarray(inputs["L0_attn_wv"], dtype=f)
    wo = np.asarray(inputs["L0_attn_wo"], dtype=f)
    gate_w = np.asarray(inputs["L0_attn_gate_w"], dtype=f)
    gate_b = np.asarray(inputs["L0_attn_gate_b"], dtype=f)
    mem_bank = np.asarray(inputs["L0_attn_mem_bank"], dtype=f)
    U = np.asarray(inputs["L0_ffn_U"], dtype=f)
    V = np.asarray(inputs["L0_ffn_V"], dtype=f)
    W1 = np.asarray(inputs["L0_ffn_W1"], dtype=f)
    b1 = np.asarray(inputs["L0_ffn_b1"], dtype=f)
    W2 = np.asarray(inputs["L0_ffn_W2"], dtype=f)
    b2 = np.asarray(inputs["L0_ffn_b2"], dtype=f)

    def blk_T(wm):
        return np.ascontiguousarray(wm.T.reshape(D, NH, HD).transpose(1, 0, 2))

    shared = {
        "wqT_h": blk_T(wq), "wkT_h": blk_T(wk), "wvT_h": blk_T(wv),
        "wk_nat": np.ascontiguousarray(wk),
        "woT": np.ascontiguousarray(wo.T),
        "mbT_pb": np.ascontiguousarray(
            mem_bank.T.reshape(DC, 128, 8, 512).transpose(2, 1, 0, 3)),
        "mb_pb": np.ascontiguousarray(
            mem_bank.reshape(32, 128, DC, 128).transpose(2, 1, 0, 3)),
        "wvT_pb": np.ascontiguousarray(
            wv.T.reshape(DC, 128, DC, 128).transpose(2, 1, 0, 3)),
        "W1_nat": np.ascontiguousarray(W1),
        "b1c": np.ascontiguousarray(b1.reshape(DC, 128).T),
        "W2r": np.ascontiguousarray(W2.reshape(DC, 128, EXP)),
        "b2b": np.ascontiguousarray(np.broadcast_to(b2[None, :], (128, EXP))),
        "gw_rep": np.ascontiguousarray(np.broadcast_to(gate_w.reshape(1, D), (128, D))),
        "gb_rep": np.full((128, 1), float(np.asarray(gate_b).reshape(-1)[0]), dtype=f),
        "VT_in": np.ascontiguousarray(V.transpose(2, 0, 1).reshape(D, EXP * R)),
        "UT_in": np.ascontiguousarray(U.transpose(0, 2, 1).reshape(EXP * R, D)),
        "iden_in": np.eye(128, dtype=f),
    }

    r = np.arange(128)[:, None]
    c = np.arange(384)[None, :]
    m_main = ((c > r) & (c <= r + WIN)).astype(f)
    mask_even = np.stack([m_main * (c >= 256), m_main * (c >= 128), m_main])
    mask_odd = np.stack([m_main, m_main, m_main])

    in_maps = []
    Tq = hs.shape[1]
    for core in range(8):
        b, half = core // 2, core % 2
        if half == 0:
            xc = np.concatenate([np.zeros((HALO, D), dtype=f), hs[b, :T]], axis=0)
            mk = mask_even
        else:
            xc = hs[b, T - HALO: Tq]
            mk = mask_odd
        m = dict(shared)
        m["x_in"] = np.ascontiguousarray(xc)
        m["mask3"] = np.ascontiguousarray(mk)
        in_maps.append(m)
    return in_maps, hs.shape


def kernel(**inputs):
    in_maps, full_shape = _prep_inputs(inputs)
    nc = build_program(debug=False)
    res = run_bass_kernel_spmd(nc, in_maps, core_ids=list(range(8)))
    B, Tq, _ = full_shape
    out = np.empty((B, Tq, D), dtype=np.float32)
    for core in range(8):
        b, half = core // 2, core % 2
        out[b, half * T:(half + 1) * T] = res.results[core]["x_out"]
    return out
